# version 14
# baseline (speedup 1.0000x reference)
import numpy as np
import ml_dtypes
import concourse.bass as bass
import concourse.mybir as mybir
from concourse.bass_utils import run_bass_kernel_spmd

F32 = mybir.dt.float32
BF16 = mybir.dt.bfloat16
I32 = mybir.dt.int32
AF = mybir.ActivationFunctionType
ALU = mybir.AluOpType
AX = mybir.AxisListType

NCORES = 8
SEQS = 2
S = 2048
D = 1024
H = 16
E = 64
DH = 4096
NT = S // 128
DC = D // 128
HC = DH // 128
VW = H * (E + 1)

ENGS = ("sync", "gpsimd", "tensor", "vector", "scalar")
USE_LNEXP_IV = False

_CACHE = {}


class Sem:
    def __init__(self, nc, name):
        self.h = nc.alloc_semaphore(name)
        self.n = 0


class Em:

    def __init__(self, nc, sems):
        self.nc = nc
        self.sems = sems
        self.reset()

    def reset(self):
        self.streams = {e: [] for e in ENGS}
        self.marks = {e: {} for e in ENGS}

    def emit(self, eng, fn, inc=None, k=1, waits=()):
        if eng == "vector":
            waits = list(waits) + [(self.sems["dv"], None)]
            if inc is None:
                inc, k = self.sems["dv"], 1
        elif eng == "scalar":
            waits = list(waits) + [(self.sems["ac"], None)]
            if inc is None:
                inc, k = self.sems["ac"], 1
        wl = []
        m = self.marks[eng]
        for sem, val in waits:
            if val is None:
                val = sem.n
            if val <= 0:
                continue
            if m.get(id(sem), -1) >= val:
                continue
            m[id(sem)] = val
            wl.append((sem.h, val))
        if inc is not None:
            inc.n += k

        def play(e, fn=fn, wl=wl, inc=inc, k=k):
            for h, v in wl:
                e.wait_ge(h, v)
            ins = fn(e)
            if inc is not None and ins is not None:
                ins.then_inc(inc.h, k)
        self.streams[eng].append(play)
        return inc.n if inc is not None else None

    def play(self):
        with self.nc.Block() as blk:
            for name in ENGS:
                fns = self.streams[name]
                if not fns:
                    continue

                def body(e, fns=fns):
                    for f in fns:
                        f(e)
                getattr(blk, name)(body)
        self.reset()


def rearr(t, spec, **kw):
    return t[:].rearrange(spec, **kw)


def rearr_ap3(t_or_ap):
    return rearr(t_or_ap, "p (c t) -> p c t", c=4)


def build_nc():
    nc = bass.Bass("TRN2", target_bir_lowering=False)

    dr = {}
    dr["x"] = nc.dram_tensor("x", [SEQS, S, D], F32, kind="ExternalInput")
    for n, shp in [("wq_t", [D, D]), ("wk_t", [D, D]), ("wv_t", [D, D]),
                   ("wo_t", [D, D]), ("w1_t", [D, DH]), ("w2_t", [DH, D]),
                   ("bq_l", [128, DC]), ("bk_l", [128, DC]),
                   ("bv_b", [128, D]), ("bo_b", [128, D]),
                   ("cw_l", [128, 3 * HC]), ("cb_l", [128, HC]),
                   ("eye32", [128, 128])]:
        dr[n] = nc.dram_tensor(n, shp, F32, kind="ExternalInput")
    dr["eye16"] = nc.dram_tensor("eye16", [128, 128], BF16, kind="ExternalInput")
    dr["out"] = nc.dram_tensor("out", [SEQS, S, D], F32, kind="ExternalOutput")
    dr["x2scr"] = nc.dram_tensor("x2scr", [SEQS, S, D], F32, kind="Internal")
    dr["fscr"] = nc.dram_tensor("fscr", [S, D], F32, kind="Internal")
    dr["w1qscr"] = nc.dram_tensor("w1qscr", [128, DC * DH], BF16, kind="Internal")
    dr["w2qscr"] = nc.dram_tensor("w2qscr", [128, HC * D], BF16, kind="Internal")

    sb = {}
    sb["ones32"] = nc.alloc_sbuf_tensor("ones32", [128, 128], F32)
    sb["ones16"] = nc.alloc_sbuf_tensor("ones16", [128, 128], BF16)
    sb["eye16"] = nc.alloc_sbuf_tensor("eye16s", [128, 128], BF16)
    sb["eye32"] = nc.alloc_sbuf_tensor("eye32s", [128, 128], F32)
    sb["bq"] = nc.alloc_sbuf_tensor("bq_sb", [128, DC], F32)
    sb["bk"] = nc.alloc_sbuf_tensor("bk_sb", [128, DC], F32)
    sb["bv"] = nc.alloc_sbuf_tensor("bv_bc", [128, D], BF16)
    sb["bo"] = nc.alloc_sbuf_tensor("bo_bc", [128, D], BF16)
    sb["cw"] = nc.alloc_sbuf_tensor("cw_sb", [128, 3 * HC], F32)
    sb["cb"] = nc.alloc_sbuf_tensor("cb_sb", [128, HC], F32)
    sb["swcol"] = nc.alloc_sbuf_tensor("swcol", [128, 2], F32)
    sb["iwcol"] = nc.alloc_sbuf_tensor("iwcol", [128, 2], F32)
    sb["invsx16"] = nc.alloc_sbuf_tensor("invsx16", [128, NT], F32)
    sb["st6"] = nc.alloc_sbuf_tensor("st6", [128, 12], F32)
    sb["st2"] = nc.alloc_sbuf_tensor("st2", [128, 2], F32)
    sb["sdc"] = nc.alloc_sbuf_tensor("sdc", [128, 8], F32)
    sb["acc2"] = nc.alloc_sbuf_tensor("acc2", [128, 2], F32)
    sb["t1c"] = nc.alloc_sbuf_tensor("t1c", [128, 2], F32)
    sb["srow"] = nc.alloc_sbuf_tensor("srow", [1, 4], F32)
    sb["epsc"] = nc.alloc_sbuf_tensor("epsc", [128, 1], F32)
    nc.const_aps.register(F32, 1e-5, sb["epsc"][:])
    sb["ivrow"] = nc.alloc_sbuf_tensor("ivrow", [1, 128], F32)

    sems = {k: Sem(nc, k) for k in
            ("scst", "gcst", "sw0", "sw1", "sx0", "sx1", "gw0", "gw1",
             "sqa", "ska", "sod", "ss0", "ss1", "sws", "sf",
             "srf0", "srf1", "srx0", "srx1", "so0", "so1",
             "pe", "dv", "ac", "gc")}
    em = Em(nc, sems)
    ctx = {"nc": nc, "dr": dr, "sb": sb, "em": em, **sems}

    ctx["use_lnexp_iv"] = USE_LNEXP_IV
    _p0(ctx)
    for s in range(SEQS):
        _seq(ctx, s)

    with nc.Block() as blk:
        @blk.sync
        def _(sync):
            sync.wait_ge(sems["so0"].h, sems["so0"].n)
            sync.wait_ge(sems["so1"].h, sems["so1"].n)
    return nc


def _p0(ctx):
    nc, dr, sb, em = ctx["nc"], ctx["dr"], ctx["sb"], ctx["em"]
    pe, dv, gc = ctx["pe"], ctx["dv"], ctx["gc"]
    scst, gcst = ctx["scst"], ctx["gcst"]
    sw = (ctx["sw0"], ctx["sw1"])

    for name, dst in [("eye16", "eye16"), ("eye32", "eye32"),
                      ("bq_l", "bq"), ("bk_l", "bk"),
                      ("cw_l", "cw"), ("cb_l", "cb")]:
        em.emit("sync", lambda e, a=dr[name], b=sb[dst]: e.dma_start(b[:], a[:]),
                inc=scst, k=16)
    for name, dst in [("bv_b", "bv"), ("bo_b", "bo")]:
        em.emit("gpsimd", lambda e, a=dr[name], b=sb[dst]: e.dma_start(b[:], a[:]),
                inc=gcst, k=16)
    em.emit("gpsimd", lambda e: e.memset(sb["ones32"][:], 1.0))
    em.emit("gpsimd", lambda e: e.memset(sb["ones16"][:], 1.0))
    em.emit("gpsimd", lambda e: e.memset(sb["epsc"][:], 1e-5))
    em.emit("gpsimd", lambda e: e.memset(sb["acc2"][:], 0.0), inc=gc)
    em.play()

    with (
        nc.sbuf_tensor("p0st", [128, 2 * DH], F32) as p0st,
        nc.psum_tensor("p0ps", [128, 8], F32) as p0ps,
    ):
        chunks = [("w1_t", c * 128, DH, 0) for c in range(DC)] + \
                 [("w2_t", c * 128, D, 1) for c in range(HC)]
        red_done = []
        for j, (src, r0, ncol, col) in enumerate(chunks):
            w = [(dv, red_done[j - 2])] if j >= 2 else []
            em.emit("sync",
                    lambda e, src=dr[src], r0=r0, ncol=ncol, j=j:
                    e.dma_start(p0st[:, (j % 2) * DH:(j % 2) * DH + ncol],
                                src[r0:r0 + 128, :]),
                    inc=sw[j % 2], k=16, waits=w)
            em.emit("vector",
                    lambda e, ncol=ncol, j=j, col=col:
                    e.reduce_sum(sb["t1c"][:, col:col + 1],
                                 p0st[:, (j % 2) * DH:(j % 2) * DH + ncol],
                                 AX.X, apply_absolute_value=True),
                    waits=[(sw[j % 2], None)])
            em.emit("vector",
                    lambda e, col=col:
                    e.tensor_add(sb["acc2"][:, col:col + 1],
                                 sb["acc2"][:, col:col + 1],
                                 sb["t1c"][:, col:col + 1]),
                    waits=[(gc, None)])
            red_done.append(dv.n)
        em.emit("tensor", lambda e: e.matmul(p0ps[0:1, 0:2], sb["ones32"][:, 0:1],
                                             sb["acc2"][:], start=True, stop=True),
                inc=pe, waits=[(dv, None)])
        em.emit("vector", lambda e: e.tensor_scalar(
            sb["srow"][0:1, 0:2], p0ps[0:1, 0:2], 1.0 / (D * DH), 1e-5,
            ALU.mult, ALU.max), waits=[(pe, None)])
        em.emit("vector", lambda e: e.reciprocal(
            sb["srow"][0:1, 2:4], sb["srow"][0:1, 0:2]))
        em.emit("tensor", lambda e: e.matmul(p0ps[:, 2:4], sb["ones32"][0:1, :],
                                             sb["srow"][0:1, 2:4],
                                             start=True, stop=True),
                inc=pe, waits=[(dv, None)])
        em.emit("tensor", lambda e: e.matmul(p0ps[:, 4:6], sb["ones32"][0:1, :],
                                             sb["srow"][0:1, 0:2],
                                             start=True, stop=True), inc=pe)
        em.emit("vector", lambda e: e.tensor_copy(sb["swcol"][:], p0ps[:, 2:4]),
                waits=[(pe, None)])
        em.emit("vector", lambda e: e.tensor_copy(sb["iwcol"][:], p0ps[:, 4:6]))
        em.play()


def _ln_stats(ctx, src_ap, w=()):
    em, sb = ctx["em"], ctx["sb"]
    em.emit("vector", lambda e: e.bn_stats(sb["st6"][:, 0:6], src_ap[:, 0:512]),
            waits=list(w))
    em.emit("vector", lambda e: e.bn_stats(sb["st6"][:, 6:12],
                                           src_ap[:, 512:1024]))
    em.emit("vector", lambda e: e.bn_aggr(sb["st2"][:], sb["st6"][:]))


def _ln_rsig(ctx):
    em, dv, sb = ctx["em"], ctx["dv"], ctx["sb"]
    em.emit("scalar", lambda e: e.activation(sb["sdc"][:, 0:1], sb["st2"][:, 1:2],
                                             AF.Sqrt, bias=1e-5),
            waits=[(dv, None)])
    em.emit("vector", lambda e: e.reciprocal(sb["sdc"][:, 1:2], sb["sdc"][:, 0:1]),
            waits=[(ctx["ac"], None)])


def _round_to_bf16(ctx, y_ap, out_ap, scr, outw=()):
    em = ctx["em"]
    em.emit("vector", lambda e: e.tensor_copy(scr["i32"], y_ap))
    em.emit("vector", lambda e: e.tensor_scalar(scr["r0"], scr["i32"], -1.0,
                                                None, ALU.add))
    em.emit("vector", lambda e: e.tensor_sub(scr["d"], y_ap, scr["r0"]))
    em.emit("vector", lambda e: e.scalar_tensor_tensor(scr["f1"], scr["d"], 1.5,
                                                       scr["r0"], ALU.is_gt,
                                                       ALU.add))
    em.emit("vector", lambda e: e.scalar_tensor_tensor(out_ap, scr["d"], 0.5,
                                                       scr["f1"], ALU.is_ge,
                                                       ALU.add),
            waits=list(outw))


def _seq(ctx, s):
    nc = ctx["nc"]
    with (
        nc.sbuf_tensor(f"s{s}_xqfm", [128, DC * S], BF16) as xqfm,
        nc.sbuf_tensor(f"s{s}_ivfm", [128, S], F32) as ivfm,
    ):
        with nc.sbuf_tensor(f"s{s}_big32", [128, DC * S], BF16) as big32:
            with (
                nc.sbuf_tensor(f"s{s}_qpk", [128, DC * S], BF16) as qpk,
                nc.sbuf_tensor(f"s{s}_kpk", [128, DC * S], BF16) as kpk,
                nc.sbuf_tensor(f"s{s}_vsb", [128, NT * VW], BF16) as vsb,
            ):
                _p1_ln1(ctx, s, big32)
                _p2_qkv(ctx, s, big32, qpk, kpk, vsb, (xqfm, [0, DC * D]))
                with (
                    nc.sbuf_tensor(f"s{s}_pt", [128, 3 * S], BF16) as pt,
                    nc.sbuf_tensor(f"s{s}_oddh", [64, S], BF16) as oddh,
                    nc.sbuf_tensor(f"s{s}_qa", [65, 2 * S], BF16) as qa,
                    nc.sbuf_tensor(f"s{s}_ka", [65, 2 * S], BF16) as ka,
                    nc.sbuf_tensor(f"s{s}_mke", [64, 65], BF16) as mke,
                ):
                    ctx["ivb16"] = xqfm
                    ctx["absq"] = oddh
                    _p3_attn(ctx, s, qpk, kpk, vsb, big32, pt, ivfm, oddh,
                             qa, ka, mke)
            _p4_proj_ln(ctx, s, big32, xqfm, ivfm)
        with (
            nc.sbuf_tensor(f"s{s}_w1q", [128, DC * DH], BF16) as w1q,
            nc.sbuf_tensor(f"s{s}_w2q", [128, HC * D], BF16) as w2q,
        ):
            _p45_wquant(ctx, s, w1q, w2q)
            _p5_ffn(ctx, s, xqfm, ivfm, w1q, w2q)


def _p1_ln1(ctx, s, xn1):
    nc, dr, sb, em = ctx["nc"], ctx["dr"], ctx["sb"], ctx["em"]
    pe, dv = ctx["pe"], ctx["dv"]
    sx = (ctx["sx0"], ctx["sx1"])
    with (
        nc.sbuf_tensor(f"s{s}_xint", [128, 2 * D], F32) as xint,
        nc.sbuf_tensor(f"s{s}_xn1t", [128, D], BF16) as xn1t,
        nc.psum_tensor(f"s{s}_p1ps", [128, D], BF16) as p1ps,
    ):
        consumed = []
        for i in range(NT):
            xb = xint[:, (i % 2) * D:(i % 2) * D + D]
            w = [(dv, consumed[i - 2])] if i >= 2 else []
            if i == 0:
                w += [(ctx["so0"], None), (ctx["so1"], None),
                      (ctx["sws"], None), (ctx["sf"], None)]
            em.emit("sync", lambda e, i=i, xb=xb: e.dma_start(
                xb, dr["x"][s, i * 128:(i + 1) * 128, :]),
                inc=sx[i % 2], k=16, waits=w)
            _ln_stats(ctx, xb, w=[(sx[i % 2], None), (ctx["scst"], None),
                                  (ctx["gcst"], None)])
            _ln_rsig(ctx)
            em.emit("vector", lambda e, xb=xb: e.tensor_scalar(
                xn1t[:], xb, sb["st2"][:, 0:1], sb["sdc"][:, 1:2],
                ALU.subtract, ALU.mult), waits=[(pe, None)])
            consumed.append(ctx["dv"].n)
            for c in range(DC):
                em.emit("tensor", lambda e, c=c: e.transpose(
                    p1ps[:, c * 128:(c + 1) * 128],
                    xn1t[:, c * 128:(c + 1) * 128], sb["eye16"][:]),
                    inc=pe if c == DC - 1 else None,
                    waits=[(dv, None)] if c == 0 else [])
            em.emit("vector", lambda e, i=i: e.tensor_copy(
                rearr(xn1, "p (c t) -> p c t", c=DC)[:, :, i * 128:(i + 1) * 128],
                rearr(p1ps, "p (c t) -> p c t", c=DC)),
                waits=[(pe, None)])
        em.play()


def _p2_qkv(ctx, s, xn1, qpk, kpk, vsb, wbufs):
    nc, dr, sb, em = ctx["nc"], ctx["dr"], ctx["sb"], ctx["em"]
    pe, dv, gc = ctx["pe"], ctx["dv"], ctx["gc"]
    gw = (ctx["gw0"], ctx["gw1"])
    wtens, wofs = wbufs

    def wload(idx, name, waits=()):
        src = rearr(dr[name], "(dc p) he -> p dc he", p=128)
        em.emit("gpsimd", lambda e, src=src, idx=idx: e.dma_start(
            rearr(wtens, "p (g dc he) -> p g dc he", g=2, dc=DC)[:, idx],
            src),
            inc=gw[idx], k=16, waits=list(waits))
        return gw[idx].n

    with (
        nc.psum_tensor(f"s{s}_qps0", [128, S], F32) as qps0,
        nc.psum_tensor(f"s{s}_qps1", [128, S], F32) as qps1,
    ):
        qps = (qps0, qps1)
        gwq = wload(0, "wq_t", waits=[(ctx["so0"], None), (ctx["so1"], None),
                                       (ctx["sws"], None), (ctx["sf"], None)])
        gwk = wload(1, "wk_t")
        gwv = [None]
        evac_done = {}
        for proj, (dst, bias, scal, wb, gwn) in enumerate(
                [(qpk, "bq", 0.125, 0, gwq), (kpk, "bk", None, 1, gwk)]):
            for oc in range(DC):
                ps = qps[oc % 2]
                w = [(gw[wb], gwn)]
                prev = (proj, oc - 2)
                if oc < 2:
                    prev = (proj - 1, oc + DC - 2)
                if prev in evac_done:
                    w.append((dv, evac_done[prev]))
                for dc in range(DC):
                    for nb in range(4):
                        em.emit("tensor", lambda e, ps=ps, oc=oc, dc=dc, nb=nb,
                                wo=wofs[wb]: e.matmul(
                            ps[:, nb * 512:(nb + 1) * 512],
                            wtens[:, wo + dc * D + oc * 128:wo + dc * D + oc * 128 + 128],
                            xn1[:, dc * S + nb * 512:dc * S + (nb + 1) * 512],
                            start=(dc == 0), stop=(dc == DC - 1)),
                            inc=pe if (dc == DC - 1 and nb == 3) else None,
                            waits=w if (dc == 0 and nb == 0) else [])
                if scal is None:
                    em.emit("vector", lambda e, ps=ps, oc=oc, dst=dst, bias=bias:
                            e.tensor_scalar(dst[:, oc * S:(oc + 1) * S], ps[:],
                                            sb[bias][:, oc:oc + 1], None,
                                            ALU.add),
                            waits=[(pe, None)])
                else:
                    em.emit("vector", lambda e, ps=ps, oc=oc, dst=dst, bias=bias,
                            sc=scal: e.tensor_scalar(
                        dst[:, oc * S:(oc + 1) * S], ps[:],
                        sb[bias][:, oc:oc + 1], sc, ALU.add, ALU.mult),
                        waits=[(pe, None)])
                evac_done[(proj, oc)] = dv.n
            if proj == 0:
                gwv[0] = wload(0, "wv_t", waits=[(pe, None)])
        em.play()

    for kt in range(NT):
        em.emit("gpsimd", lambda e, kt=kt: e.memset(
            rearr(vsb, "p (kt h e) -> p kt h e", kt=NT, h=H)[:, kt, :, 64:65],
            1.0), inc=gc if kt == NT - 1 else None)
    with (
        nc.psum_tensor(f"s{s}_vps0", [128, D], F32) as vps0,
        nc.psum_tensor(f"s{s}_vps1", [128, D], F32) as vps1,
    ):
        vps = (vps0, vps1)
        evd = []
        for tt in range(NT):
            ps = vps[tt % 2]
            w = [(ctx["gw0"], gwv[0])]
            if tt >= 2:
                w.append((dv, evd[tt - 2]))
            for dc in range(DC):
                for nb in range(2):
                    em.emit("tensor", lambda e, ps=ps, tt=tt, dc=dc, nb=nb:
                            e.matmul(
                        ps[:, nb * 512:(nb + 1) * 512],
                        xn1[:, dc * S + tt * 128:dc * S + tt * 128 + 128],
                        wtens[:, wofs[0] + dc * D + nb * 512:wofs[0] + dc * D + (nb + 1) * 512],
                        start=(dc == 0), stop=(dc == DC - 1)),
                        inc=pe if (dc == DC - 1 and nb == 1) else None,
                        waits=w if (dc == 0 and nb == 0) else [])
            em.emit("vector", lambda e, ps=ps, tt=tt: e.tensor_tensor(
                rearr(vsb, "p (kt h e) -> p kt h e", kt=NT, h=H)[:, tt, :, 0:64],
                rearr(ps, "p (h e) -> p h e", h=H),
                rearr(sb["bv"], "p (h e) -> p h e", h=H), ALU.add),
                waits=[(pe, None), (gc, None)])
            evd.append(dv.n)
        em.play()


def _p3_attn(ctx, s, qpk, kpk, vsb, attn, pt, ivr, oddh, qa, ka, mke):
    nc, sb, em = ctx["nc"], ctx["sb"], ctx["em"]
    pe, dv, ac, gc = ctx["pe"], ctx["dv"], ctx["ac"], ctx["gc"]
    sqa, ska, sod = ctx["sqa"], ctx["ska"], ctx["sod"]
    absq = ctx["absq"]
    Q = 1024

    with (
        nc.psum_tensor(f"s{s}_sc0", [128, Q], F32) as sc0,
        nc.psum_tensor(f"s{s}_sc1", [128, Q], F32) as sc1,
        nc.psum_tensor(f"s{s}_atps", [65, S], F32) as atps,
    ):
        scp = (sc0, sc1)
        em.emit("gpsimd", lambda e: e.memset(ka[64:65, :], 1.0))
        em.emit("gpsimd", lambda e: e.memset(mke[:], 0.0), inc=gc)

        odd_dma = {}
        attv_last = {}
        attv_cnt = {}
        exp_cnt = {}
        mult_done = {}
        sc_cnt = {}

        def aug_prep(h):
            if h >= H:
                return
            c, half, b = h // 2, h % 2, h % 2
            qsl = slice(c * S, (c + 1) * S)
            wbuf = []
            if h >= 2:
                wbuf.append((pe, attv_last[h - 2]))
            for src, dstt, dsem in ((qpk, qa, sqa), (kpk, ka, ska)):
                if half == 0:
                    em.emit("vector", lambda e, src=src, dstt=dstt, qsl=qsl, b=b:
                            e.tensor_copy(dstt[0:64, b * S:(b + 1) * S],
                                          src[0:64, qsl]),
                            waits=wbuf)
                else:
                    em.emit("sync", lambda e, src=src, dstt=dstt, qsl=qsl, b=b:
                            e.dma_start(dstt[0:64, b * S:(b + 1) * S],
                                        src[64:128, qsl]),
                            inc=dsem, k=16, waits=wbuf)
            em.emit("vector", lambda e, b=b: e.reduce_max(
                mke[:, 64:65], ka[0:64, b * S:(b + 1) * S], AX.X,
                apply_absolute_value=True),
                waits=[(ska, None), (gc, None), (pe, None)])

        def emit_absq(h):
            if h >= H:
                return
            b = h % 2
            em.emit("vector", lambda e, b=b: e.tensor_scalar_mul(
                absq[0:64, :], qa[0:64, b * S:(b + 1) * S], -1.0),
                waits=[(sqa, None), (sod, None)])
            em.emit("vector", lambda e, b=b: e.tensor_tensor(
                absq[0:64, :], absq[0:64, :], qa[0:64, b * S:(b + 1) * S],
                ALU.max))

        def emit_M(h, j):
            for nb in range(2):
                em.emit("tensor", lambda e, nb=nb, j=j: e.matmul(
                    scp[j][0:65, nb * 512:(nb + 1) * 512], mke[:],
                    absq[0:64, j * Q + nb * 512:j * Q + (nb + 1) * 512],
                    start=True, stop=True),
                    inc=pe if nb == 1 else None,
                    waits=([(dv, None), (ac, exp_cnt.get((h - 1, NT - 1, j), 0))]
                           if nb == 0 else []))
            b = h % 2
            em.emit("vector", lambda e, b=b, j=j: e.tensor_scalar(
                qa[64:65, b * S + j * Q:b * S + (j + 1) * Q],
                scp[j][64:65, :], 80.0, -1.0,
                ALU.min, ALU.mult), waits=[(pe, None)])
            return dv.n

        def emit_scores(h, kt, j, negm):
            b = h % 2
            for nb in range(2):
                w = []
                if nb == 0:
                    if kt == 0:
                        w = [(dv, negm[j])]
                    else:
                        w = [(ac, exp_cnt[(h, kt - 1, j)])]
                    if kt == 7:
                        w.append((dv, mult_done.get(h - 1, 0)))
                em.emit("tensor", lambda e, kt=kt, nb=nb, b=b, j=j: e.matmul(
                    scp[j][:, nb * 512:(nb + 1) * 512],
                    ka[:, b * S + kt * 128:b * S + kt * 128 + 128],
                    qa[:, b * S + j * Q + nb * 512:b * S + j * Q + (nb + 1) * 512],
                    start=True, stop=True),
                    inc=pe if nb == 1 else None, waits=w)
            sc_cnt[(h, kt, j)] = pe.n

        def emit_exp(h, kt, j):
            p = kt % 3
            w = [(pe, sc_cnt[(h, kt, j)])]
            if kt >= 3:
                w.append((pe, attv_cnt.get((h, kt - 3, j), 0)))
            em.emit("scalar", lambda e, p=p, j=j: e.activation(
                pt[:, p * S + j * Q:p * S + (j + 1) * Q], scp[j][:], AF.Exp),
                waits=w)
            exp_cnt[(h, kt, j)] = ac.n

        def emit_attv(h, kt, j):
            p = kt % 3
            for nb in range(2):
                w = []
                if nb == 0:
                    w = [(ac, exp_cnt[(h, kt, j)])]
                    if kt == 0:
                        w.append((dv, ctx.get(f"_cpA{h - 1}", 0)))
                em.emit("tensor", lambda e, kt=kt, nb=nb, h=h, p=p, j=j: e.matmul(
                    atps[0:65, j * Q + nb * 512:j * Q + (nb + 1) * 512],
                    vsb[:, kt * VW + h * 65:kt * VW + (h + 1) * 65],
                    pt[:, p * S + j * Q + nb * 512:p * S + j * Q + (nb + 1) * 512],
                    start=(kt == 0), stop=(kt == NT - 1)),
                    inc=pe if nb == 1 else None, waits=w)
            attv_cnt[(h, kt, j)] = pe.n

        def emit_evac(h):
            em.emit("vector", lambda e: e.tensor_copy(ivr[0:65, :],
                                                      atps[0:65, :]),
                    waits=[(pe, attv_cnt[(h, NT - 1, 1)])])
            ctx[f"_cpA{h}"] = dv.n

        def emit_recip(h):
            if h < 0:
                return
            em.emit("vector", lambda e: e.reciprocal(
                ivr[64:65, :], ivr[64:65, :]))
            em.emit("vector", lambda e: e.tensor_copy(
                ctx["ivb16"][64:65, 0:S], ivr[64:65, :]))
            ctx[f"_iv{h}"] = dv.n

        def emit_post(h):
            if h < 0:
                return
            c, half = h // 2, h % 2
            for j in range(2):
                for nb in range(2):
                    w = []
                    if nb == 0:
                        w = [(dv, ctx[f"_iv{h}"]), (ac, None)]
                    em.emit("tensor", lambda e, nb=nb, j=j: e.matmul(
                        scp[j][0:64, nb * 512:(nb + 1) * 512],
                        sb["ones32"][64:65, 0:64],
                        ivr[64:65, j * Q + nb * 512:j * Q + (nb + 1) * 512],
                        start=True, stop=True),
                        inc=pe if nb == 1 else None, waits=w)
                dst = attn[0:64, c * S + j * Q:c * S + (j + 1) * Q] \
                    if half == 0 else oddh[0:64, j * Q:(j + 1) * Q]
                ww = [(pe, None)]
                if half == 1 and j == 0:
                    ww.append((sod, odd_dma.get(h - 2, 0)))
                em.emit("vector", lambda e, dst=dst, j=j: e.tensor_tensor(
                    dst, ivr[0:64, j * Q:(j + 1) * Q], scp[j][0:64, :],
                    ALU.mult), waits=ww)
            if half == 1:
                em.emit("sync", lambda e, c=c: e.dma_start(
                    attn[64:128, c * S:(c + 1) * S], oddh[0:64, :]),
                    inc=sod, k=16, waits=[(dv, None)])
                odd_dma[h] = sod.n
            mult_done[h] = dv.n

        aug_prep(0)
        emit_absq(0)
        for h in range(H):
            negm = [emit_M(h, 0), emit_M(h, 1)]
            aug_prep(h + 1)
            emit_recip(h - 1)
            for kt in range(3):
                for j in range(2):
                    emit_scores(h, kt, j, negm)
                    emit_exp(h, kt, j)
            emit_attv(h, 0, 0)
            emit_attv(h, 0, 1)
            for kt in range(3, NT):
                if kt == 7:
                    emit_post(h - 1)
                    emit_absq(h + 1)
                for j in range(2):
                    emit_scores(h, kt, j, negm)
                    emit_exp(h, kt, j)
                    emit_attv(h, kt - 2, j)
            for j in range(2):
                emit_attv(h, NT - 2, j)
                emit_attv(h, NT - 1, j)
            attv_last[h] = attv_cnt[(h, NT - 1, 1)]
            emit_evac(h)
        emit_recip(H - 1)
        emit_post(H - 1)
        em.play()
    ctx["attn_sod"] = ctx["sod"].n
def _p4_proj_ln(ctx, s, attn, xqfm, ivfm):
    nc, dr, sb, em = ctx["nc"], ctx["dr"], ctx["sb"], ctx["em"]
    pe, dv, ac = ctx["pe"], ctx["dv"], ctx["ac"]
    sx = (ctx["sx0"], ctx["sx1"])
    ss = (ctx["ss0"], ctx["ss1"])
    gw1 = ctx["gw1"]
    with (
        nc.sbuf_tensor(f"s{s}_wo", [128, DC * D], BF16) as wo,
        nc.sbuf_tensor(f"s{s}_xint2", [128, 2 * D], F32) as xint2,
        nc.sbuf_tensor(f"s{s}_ao", [128, D], F32) as ao,
        nc.sbuf_tensor(f"s{s}_x2t", [128, 2 * D], F32) as x2t,
        nc.sbuf_tensor(f"s{s}_xn2", [128, D], F32) as xn2,
        nc.sbuf_tensor(f"s{s}_yq", [128, D], F32) as yq,
        nc.sbuf_tensor(f"s{s}_qi32", [128, D], I32) as qi32,
        nc.sbuf_tensor(f"s{s}_r0f", [128, D], F32) as r0f,
        nc.sbuf_tensor(f"s{s}_dbf", [128, D], F32) as dbf,
        nc.sbuf_tensor(f"s{s}_f1f", [128, D], F32) as f1f,
        nc.sbuf_tensor(f"s{s}_xqt", [128, D], BF16) as xqt,
        nc.psum_tensor(f"s{s}_op0", [128, D], F32) as op0,
        nc.psum_tensor(f"s{s}_op1", [128, D], F32) as op1,
        nc.psum_tensor(f"s{s}_trp", [128, D], BF16) as trp,
        nc.psum_tensor(f"s{s}_ivt", [1, 128], F32) as ivt,
        nc.psum_tensor(f"s{s}_ivb", [128, 128], F32) as ivb,
    ):
        em.emit("gpsimd", lambda e: e.dma_start(
            rearr(wo, "p (c d) -> p c d", c=DC),
            rearr(dr["wo_t"], "(c p) d -> p c d", p=128)), inc=gw1, k=16,
            waits=[(ctx["sod"], ctx["attn_sod"]), (ctx["sqa"], None),
                   (ctx["ska"], None)])
        scr = {"i32": qi32[:], "r0": r0f[:], "d": dbf[:], "f1": f1f[:]}
        ops = (op0, op1)
        x_used = []
        evac_done = []
        spill_sp = []
        pend_spill = []
        for i in range(NT):
            xb = xint2[:, (i % 2) * D:(i % 2) * D + D]
            w = [(dv, x_used[i - 2])] if i >= 2 else []
            em.emit("sync", lambda e, i=i, xb=xb: e.dma_start(
                xb, dr["x"][s, i * 128:(i + 1) * 128, :]),
                inc=sx[i % 2], k=16, waits=w)
            ps = ops[i % 2]
            w = [(gw1, None), (ctx["sod"], ctx["attn_sod"])]
            if i >= 2:
                w.append((dv, evac_done[i - 2]))
            for c in range(DC):
                for nb in range(2):
                    em.emit("tensor", lambda e, ps=ps, i=i, c=c, nb=nb: e.matmul(
                        ps[:, nb * 512:(nb + 1) * 512],
                        attn[:, c * S + i * 128:c * S + i * 128 + 128],
                        wo[:, c * D + nb * 512:c * D + (nb + 1) * 512],
                        start=(c == 0), stop=(c == DC - 1)),
                        inc=pe if (c == DC - 1 and nb == 1) else None,
                        waits=w if (c == 0 and nb == 0) else [])
            em.emit("vector", lambda e, ps=ps: e.tensor_tensor(
                ao[:], ps[:], sb["bo"][:], ALU.add), waits=[(pe, None)])
            evac_done.append(dv.n)
            _ln_stats(ctx, ao[:])
            _ln_rsig(ctx)
            em.emit("vector", lambda e: e.tensor_scalar(
                yq[:], ao[:], sb["st2"][:, 0:1], sb["sdc"][:, 1:2],
                ALU.subtract, ALU.mult))
            x2b = x2t[:, (i % 2) * D:(i % 2) * D + D]
            xw = [(sx[i % 2], None)]
            if i >= 2:
                xw.append((ss[i % 2], spill_sp[i - 2]))
            em.emit("vector", lambda e, xb=xb, x2b=x2b: e.tensor_add(
                x2b, yq[:], xb), waits=xw)
            x_used.append(dv.n)
            pend_spill.append((i, x2b, dv.n))
            if len(pend_spill) > 1 or i == NT - 1:
                for (pi, px, pdv) in pend_spill[:None if i == NT - 1 else -1]:
                    em.emit("sync", lambda e, pi=pi, px=px: e.dma_start(
                        dr["x2scr"][s, pi * 128:(pi + 1) * 128, :], px),
                        inc=ss[pi % 2], k=16, waits=[(dv, pdv)])
                    spill_sp.append(ss[pi % 2].n)
                pend_spill[:] = ([] if i == NT - 1 else pend_spill[-1:])
            _ln_stats(ctx, x2b)
            _ln_rsig(ctx)
            em.emit("vector", lambda e, x2b=x2b: e.tensor_scalar(
                xn2[:], x2b, sb["st2"][:, 0:1], sb["sdc"][:, 1:2],
                ALU.subtract, ALU.mult))
            em.emit("vector", lambda e: e.reduce_max(
                sb["sdc"][:, 2:3], xn2[:], AX.X, apply_absolute_value=True))
            em.emit("vector", lambda e: e.tensor_scalar_max(
                sb["sdc"][:, 3:4], sb["sdc"][:, 2:3], 1e-5))
            em.emit("vector", lambda e: e.reciprocal(
                sb["sdc"][:, 4:5], sb["sdc"][:, 3:4]))
            em.emit("vector", lambda e: e.tensor_scalar_mul(
                sb["sdc"][:, 5:6], sb["sdc"][:, 4:5], 127.0))
            em.emit("vector", lambda e, i=i: e.tensor_scalar_mul(
                sb["invsx16"][:, i:i + 1], sb["sdc"][:, 3:4], 1.0 / 127.0))
            em.emit("vector", lambda e: e.tensor_scalar_mul(
                yq[:], xn2[:], sb["sdc"][:, 5:6]))
            _round_to_bf16(ctx, yq[:], xqt[:], scr, outw=[(pe, None)])
            for c in range(DC):
                em.emit("tensor", lambda e, c=c: e.transpose(
                    trp[:, c * 128:(c + 1) * 128], xqt[:, c * 128:(c + 1) * 128],
                    sb["eye16"][:]), inc=pe if c == DC - 1 else None,
                    waits=[(dv, None)] if c == 0 else [])
            em.emit("vector", lambda e, i=i: e.tensor_copy(
                rearr(xqfm, "p (c t) -> p c t", c=DC)[:, :, i * 128:(i + 1) * 128],
                rearr(trp, "p (c t) -> p c t", c=DC)), waits=[(pe, None)])
            em.emit("tensor", lambda e, i=i: e.transpose(
                ivt[0:1, :], sb["invsx16"][:, i:i + 1], sb["eye32"][:]),
                inc=pe, waits=[(dv, None)])
            em.emit("vector", lambda e: e.tensor_copy(sb["ivrow"][0:1, :],
                                                      ivt[0:1, :]),
                    waits=[(pe, None)])
            em.emit("tensor", lambda e: e.matmul(
                ivb[:], sb["ones32"][0:1, :], sb["ivrow"][0:1, :],
                start=True, stop=True), inc=pe, waits=[(dv, None)])
            em.emit("vector", lambda e, i=i: e.tensor_copy(
                ivfm[:, i * 128:(i + 1) * 128], ivb[:]), waits=[(pe, None)])
        em.play()


def _p45_wquant(ctx, s, w1q, w2q):
    nc, dr, sb, em = ctx["nc"], ctx["dr"], ctx["sb"], ctx["em"]
    dv, sws = ctx["dv"], ctx["sws"]
    sw = (ctx["sw0"], ctx["sw1"])
    if s == 1:
        em.emit("sync", lambda e: e.dma_start(w1q[:], dr["w1qscr"][:]),
                inc=sws, k=16,
                waits=[(sws, 32), (ctx["ss0"], None), (ctx["ss1"], None),
                       (ctx["sx0"], None), (ctx["sx1"], None)])
        em.emit("sync", lambda e: e.dma_start(w2q[:], dr["w2qscr"][:]),
                inc=sws, k=16)
        em.play()
        ctx["wq_rdy"] = sws.n
        return
    with (
        nc.sbuf_tensor(f"s{s}_wst", [128, 2 * 2048], F32) as wst,
        nc.sbuf_tensor(f"s{s}_c1b", [128, 2048], BF16) as c1b,
        nc.sbuf_tensor(f"s{s}_c2b", [128, 2048], BF16) as c2b,
    ):
        jobs = []
        for c in range(DC):
            for hcol in range(2):
                jobs.append(("w1_t", c * 128, hcol * 2048, 2048, 0,
                             w1q, c * DH + hcol * 2048))
        for c in range(HC):
            jobs.append(("w2_t", c * 128, 0, D, 1, w2q, c * D))
        qdone = []
        for j, (src, r0, c0, ncol, scol, dst, dcol) in enumerate(jobs):
            stg = wst[:, (j % 2) * 2048:(j % 2) * 2048 + ncol]
            w = [(dv, qdone[j - 2])] if j >= 2 else []
            if j == 0:
                w += [(ctx["ss0"], None), (ctx["ss1"], None),
                      (ctx["sx0"], None), (ctx["sx1"], None)]
            em.emit("sync", lambda e, stg=stg, src=src, r0=r0, c0=c0, ncol=ncol:
                    e.dma_start(stg, dr[src][r0:r0 + 128, c0:c0 + ncol]),
                    inc=sw[j % 2], k=16, waits=w)
            em.emit("vector", lambda e, stg=stg, ncol=ncol, scol=scol:
                    e.tensor_scalar(c1b[:, 0:ncol], stg,
                                    sb["swcol"][:, scol:scol + 1], 0.5,
                                    ALU.mult, ALU.is_ge),
                    waits=[(sw[j % 2], None)])
            em.emit("vector", lambda e, stg=stg, ncol=ncol, scol=scol:
                    e.tensor_scalar(c2b[:, 0:ncol], stg,
                                    sb["swcol"][:, scol:scol + 1], -0.5,
                                    ALU.mult, ALU.is_gt))
            em.emit("vector", lambda e, dst=dst, dcol=dcol, ncol=ncol:
                    e.scalar_tensor_tensor(dst[:, dcol:dcol + ncol],
                                           c1b[:, 0:ncol], -1.0,
                                           c2b[:, 0:ncol], ALU.add, ALU.add))
            qdone.append(dv.n)
        em.emit("sync", lambda e: e.dma_start(dr["w1qscr"][:], w1q[:]),
                inc=sws, k=16, waits=[(dv, None)])
        em.emit("sync", lambda e: e.dma_start(dr["w2qscr"][:], w2q[:]),
                inc=sws, k=16)
        em.play()
        ctx["wq_rdy"] = None


def _p5_ffn(ctx, s, xqfm, ivfm, w1q, w2q):
    nc, dr, sb, em = ctx["nc"], ctx["dr"], ctx["sb"], ctx["em"]
    pe, dv, ac = ctx["pe"], ctx["dv"], ctx["ac"]
    sf, sws = ctx["sf"], ctx["sws"]
    fspill = []
    with (
        nc.sbuf_tensor(f"s{s}_hgr", [128, 2 * 4 * 130], BF16) as hgr,
        nc.sbuf_tensor(f"s{s}_tmpg", [128, 512], BF16) as tmpg,
        nc.sbuf_tensor(f"s{s}_tcv", [128, HC * 128], BF16) as tcv,
        nc.sbuf_tensor(f"s{s}_gq", [128, 2 * 512], BF16) as gqb,
        nc.sbuf_tensor(f"s{s}_gmax", [128, 128], F32) as gmax,
        nc.sbuf_tensor(f"s{s}_sgb", [128, 128], F32) as sgb,
        nc.sbuf_tensor(f"s{s}_yb", [128, 512], F32) as yb,
        nc.sbuf_tensor(f"s{s}_qi2", [128, 512], I32) as qi2,
        nc.sbuf_tensor(f"s{s}_r02", [128, 512], F32) as r02,
        nc.sbuf_tensor(f"s{s}_db2", [128, 512], F32) as db2,
        nc.sbuf_tensor(f"s{s}_f12", [128, 512], F32) as f12,
        nc.sbuf_tensor(f"s{s}_ft", [128, D], F32) as ft,
        nc.psum_tensor(f"s{s}_hps0", [128, 130], F32) as hps0,
        nc.psum_tensor(f"s{s}_hps1", [128, 130], F32) as hps1,
        nc.psum_tensor(f"s{s}_fp0", [128, D], F32) as fp0,
        nc.psum_tensor(f"s{s}_fp1", [128, D], F32) as fp1,
        nc.psum_tensor(f"s{s}_tmp", [128, 128], F32) as tmps,
    ):
        hps = (hps0, hps1)
        fps = (fp0, fp1)
        scr = {"i32": qi2[:], "r0": r02[:], "d": db2[:], "f1": f12[:]}
        fevac = []
        hevac = []
        convd = []
        w2done = {}
        for i in range(NT):
            j0 = 1 if i == 0 else 0
            j1 = 129 if i == NT - 1 else 130
            t0 = i * 128 - 1
            for cc in range(HC):
                idx = i * HC + cc
                g, c4 = cc // 4, cc % 4
                ps = hps[cc % 2]
                if idx >= 2:
                    w = [(dv, hevac[idx - 2])]
                else:
                    w = [(dv, None)]
                    if ctx.get("wq_rdy"):
                        w.append((sws, ctx["wq_rdy"]))
                for dc in range(DC):
                    em.emit("tensor", lambda e, ps=ps, cc=cc, dc=dc,
                            j0=j0, j1=j1, t0=t0: e.matmul(
                        ps[:, j0:j1],
                        w1q[:, dc * DH + cc * 128:dc * DH + cc * 128 + 128],
                        xqfm[:, dc * S + t0 + j0:dc * S + t0 + j1],
                        start=(dc == 0), stop=(dc == DC - 1)),
                        inc=pe if dc == DC - 1 else None,
                        waits=w if dc == 0 else [])
                gb = (g % 2) * 520
                hw = [(pe, None)]
                if len(convd) >= 2:
                    hw.append((dv, convd[-2]))
                em.emit("vector", lambda e, ps=ps, c4=c4, gb=gb, j0=j0, j1=j1,
                        t0=t0: e.scalar_tensor_tensor(
                    hgr[:, gb + c4 * 130 + j0:gb + c4 * 130 + j1],
                    ps[:, j0:j1], sb["iwcol"][:, 0:1],
                    ivfm[:, t0 + j0:t0 + j1], ALU.mult, ALU.mult),
                    waits=hw)
                if i == 0:
                    em.emit("vector", lambda e, c4=c4, gb=gb: e.memset(
                        hgr[:, gb + c4 * 130:gb + c4 * 130 + 1], 0.0))
                if i == NT - 1:
                    em.emit("vector", lambda e, c4=c4, gb=gb: e.memset(
                        hgr[:, gb + c4 * 130 + 129:gb + c4 * 130 + 130], 0.0))
                hevac.append(dv.n)
                if c4 == 3:
                    def hv(j, gb=gb):
                        return bass.AP(hgr, gb + j, [[1040, 128], [130, 4],
                                                     [1, 128]])
                    def wmap(j, g=g):
                        return bass.AP(sb["cw"], 3 * (4 * g) + j,
                                       [[3 * HC, 128], [3, 4], [0, 128]])
                    cbm = bass.AP(sb["cb"], 4 * g,
                                  [[HC, 128], [1, 4], [0, 128]])
                    out3 = bass.AP(tcv, g * 512, [[HC * 128, 128], [128, 4],
                                                  [1, 128]])
                    tm3 = bass.AP(tmpg, 0, [[512, 128], [128, 4], [1, 128]])
                    em.emit("vector", lambda e, hv=hv, wmap=wmap, out3=out3:
                            e.tensor_tensor(out3, hv(1), wmap(1), ALU.mult))
                    em.emit("vector", lambda e, cbm=cbm, out3=out3:
                            e.tensor_tensor(out3, out3, cbm, ALU.add))
                    em.emit("vector", lambda e, hv=hv, wmap=wmap, tm3=tm3:
                            e.tensor_tensor(tm3, hv(0), wmap(0), ALU.mult))
                    em.emit("vector", lambda e, out3=out3, tm3=tm3:
                            e.tensor_tensor(out3, out3, tm3, ALU.add))
                    em.emit("vector", lambda e, hv=hv, wmap=wmap, tm3=tm3:
                            e.tensor_tensor(tm3, hv(2), wmap(2), ALU.mult))
                    em.emit("vector", lambda e, out3=out3, tm3=tm3:
                            e.tensor_tensor(out3, out3, tm3, ALU.add))
                    convd.append(dv.n)
                    g_ = cc // 4
                    em.emit("scalar", lambda e, g_=g_: e.activation(
                        tcv[:, g_ * 512:(g_ + 1) * 512],
                        tcv[:, g_ * 512:(g_ + 1) * 512], AF.Silu),
                        waits=[(dv, None)])
                    em.emit("vector", lambda e, g_=g_: e.tensor_reduce(
                        gmax[:] if g_ == 0 else f12[:, 0:128],
                        bass.AP(tcv, g_ * 512, [[HC * 128, 128], [1, 128],
                                                [128, 4]]),
                        AX.X, ALU.max, apply_absolute_value=True),
                        waits=[(ac, None)])
                    if g_ > 0:
                        em.emit("vector", lambda e: e.tensor_max(
                            gmax[:], gmax[:], f12[:, 0:128]))
            em.emit("tensor", lambda e: e.transpose(
                tmps[:], gmax[:], sb["eye32"][:]), inc=pe,
                waits=[(dv, None), (dv, None)])
            em.emit("vector", lambda e: e.reduce_max(
                sb["sdc"][:, 2:3], tmps[:], AX.X), waits=[(pe, None)])
            em.emit("vector", lambda e: e.tensor_scalar_max(
                sb["sdc"][:, 3:4], sb["sdc"][:, 2:3], 1e-5))
            em.emit("vector", lambda e: e.reciprocal(
                sb["sdc"][:, 4:5], sb["sdc"][:, 3:4]))
            em.emit("vector", lambda e: e.tensor_scalar_mul(
                sb["sdc"][:, 5:6], sb["sdc"][:, 4:5], 127.0))
            em.emit("vector", lambda e: e.tensor_scalar_mul(
                sb["sdc"][:, 6:7], sb["sdc"][:, 3:4], 1.0 / 127.0))
            em.emit("tensor", lambda e: e.transpose(
                tmps[0:1, :], sb["sdc"][:, 5:6], sb["eye32"][:]), inc=pe,
                waits=[(dv, None)])
            em.emit("vector", lambda e: e.tensor_copy(
                sb["ivrow"][0:1, :], tmps[0:1, :]), waits=[(pe, None)])
            em.emit("tensor", lambda e: e.matmul(
                tmps[:], sb["ones32"][0:1, :], sb["ivrow"][0:1, :],
                start=True, stop=True), inc=pe, waits=[(dv, None)])
            em.emit("vector", lambda e: e.tensor_copy(sgb[:], tmps[:]),
                    waits=[(pe, None)])
            sgv = bass.AP(sgb, 0, [[128, 128], [0, 4], [1, 128]])
            silu_ac = ctx["ac"].n
            sgb_dv = dv.n
            del silu_ac
            silu_ac = ctx["ac"].n
            dg = 0
            for g in range(8):
                gq4 = gqb[:, (g % 2) * 512:(g % 2) * 512 + 512]
                ts = tcv[:, g * 512:(g + 1) * 512]
                yw = [(ac, silu_ac), (dv, sgb_dv), (ctx["gc"], dg)]
                if i >= 1 and g == 0:
                    yw.append((sf, fspill[i - 1]))
                em.emit("gpsimd", lambda e, ts=ts: e.tensor_tensor(
                    rearr_ap3(yb), rearr_ap3(ts), sgv, ALU.mult),
                    inc=ctx["gc"], waits=yw)
                yg = ctx["gc"].n
                em.emit("vector", lambda e: e.tensor_copy(qi2[:], yb[:]),
                        waits=[(ctx["gc"], yg)])
                em.emit("vector", lambda e: e.tensor_scalar(
                    r02[:], qi2[:], -1.0, None, ALU.add))
                r0d = dv.n
                em.emit("gpsimd", lambda e: e.tensor_sub(db2[:], yb[:], r02[:]),
                        inc=ctx["gc"], waits=[(dv, r0d)])
                dg = ctx["gc"].n
                em.emit("vector", lambda e: e.scalar_tensor_tensor(
                    f12[:], db2[:], 1.5, r02[:], ALU.is_gt, ALU.add),
                    waits=[(ctx["gc"], dg)])
                ow = []
                if (i, g - 2) in w2done:
                    ow.append((pe, w2done[(i, g - 2)]))
                em.emit("vector", lambda e, gq4=gq4: e.scalar_tensor_tensor(
                    gq4, db2[:], 0.5, f12[:], ALU.is_ge, ALU.add), waits=ow)
                qd = dv.n
                ps = fps[i % 2]
                for cc4 in range(4):
                    cc = g * 4 + cc4
                    w2w = [(dv, qd)]
                    if i >= 2 and cc == 0:
                        w2w.append((dv, fevac[i - 2]))
                    for nb in range(2):
                        em.emit("tensor", lambda e, ps=ps, cc=cc, nb=nb,
                                gq4=gq4, cc4=cc4: e.matmul(
                            ps[:, nb * 512:(nb + 1) * 512],
                            gq4[:, cc4 * 128:(cc4 + 1) * 128],
                            w2q[:, cc * D + nb * 512:cc * D + (nb + 1) * 512],
                            start=(cc == 0), stop=(cc == HC - 1)),
                            inc=pe if nb == 1 else None,
                            waits=w2w if (nb == 0 and cc4 == 0) else [])
                w2done[(i, g)] = pe.n
            fw = [(pe, None)]
            if i >= 1:
                fw.append((sf, fspill[i - 1]))
            em.emit("vector", lambda e, i=i: e.tensor_scalar(
                ft[:], fps[i % 2][:], sb["sdc"][:, 6:7], sb["iwcol"][:, 1:2],
                ALU.mult, ALU.mult), waits=fw)
            fevac.append(dv.n)
            em.emit("gpsimd", lambda e, i=i: e.dma_start(
                dr["fscr"][i * 128:(i + 1) * 128, :], ft[:]),
                inc=sf, k=16, waits=[(dv, None)])
            fspill.append(sf.n)
        em.play()

    with (
        nc.sbuf_tensor(f"s{s}_frl", [128, 2 * D], F32) as frl,
        nc.sbuf_tensor(f"s{s}_x2rl", [128, 2 * D], F32) as x2rl,
        nc.sbuf_tensor(f"s{s}_fn", [128, D], F32) as fn,
        nc.sbuf_tensor(f"s{s}_outt", [128, 2 * D], F32) as outt,
    ):
        srf = (ctx["srf0"], ctx["srf1"])
        srx = (ctx["srx0"], ctx["srx1"])
        so = (ctx["so0"], ctx["so1"])
        used = []
        outdma = []
        for i in range(NT):
            fb = frl[:, (i % 2) * D:(i % 2) * D + D]
            xb = x2rl[:, (i % 2) * D:(i % 2) * D + D]
            w = [(sf, fspill[NT - 1]),
                 (ctx["ss0"], None), (ctx["ss1"], None)]
            if i >= 2:
                w.append((dv, used[i - 2]))
            em.emit("sync", lambda e, i=i, fb=fb: e.dma_start(
                fb, dr["fscr"][i * 128:(i + 1) * 128, :]),
                inc=srf[i % 2], k=16, waits=w)
            em.emit("sync", lambda e, i=i, xb=xb: e.dma_start(
                xb, dr["x2scr"][s, i * 128:(i + 1) * 128, :]),
                inc=srx[i % 2], k=16)
            _ln_stats(ctx, fb, w=[(srf[i % 2], None)])
            _ln_rsig(ctx)
            em.emit("vector", lambda e, fb=fb: e.tensor_scalar(
                fn[:], fb, sb["st2"][:, 0:1], sb["sdc"][:, 1:2],
                ALU.subtract, ALU.mult))
            ow = [(srx[i % 2], None)]
            if i >= 2:
                ow.append((so[i % 2], outdma[i - 2]))
            em.emit("vector", lambda e, xb=xb, i=i: e.tensor_add(
                outt[:, (i % 2) * D:(i % 2) * D + D], fn[:], xb), waits=ow)
            used.append(dv.n)
            em.emit("gpsimd", lambda e, i=i: e.dma_start(
                dr["out"][s, i * 128:(i + 1) * 128, :],
                outt[:, (i % 2) * D:(i % 2) * D + D]),
                inc=so[i % 2], k=16, waits=[(dv, None)])
            outdma.append(so[i % 2].n)
        em.play()


def _host_inputs(inputs):
    f32 = np.float32
    wq = np.ascontiguousarray(
        np.asarray(inputs["L0_attn_wq"], f32).transpose(1, 0, 2).reshape(D, D))
    wk = np.ascontiguousarray(
        np.asarray(inputs["L0_attn_wk"], f32).transpose(1, 0, 2).reshape(D, D))
    wv = np.ascontiguousarray(
        np.asarray(inputs["L0_attn_wv"], f32).transpose(1, 0, 2).reshape(D, D))
    wo = np.ascontiguousarray(np.asarray(inputs["L0_attn_wo"], f32).T)
    w1 = np.ascontiguousarray(np.asarray(inputs["L0_ffn_w1"], f32).T)
    w2 = np.ascontiguousarray(np.asarray(inputs["L0_ffn_w2"], f32).T)
    bq = np.ascontiguousarray(
        np.asarray(inputs["L0_attn_bq"], f32).reshape(D).reshape(DC, 128).T)
    bk = np.ascontiguousarray(
        np.asarray(inputs["L0_attn_bk"], f32).reshape(D).reshape(DC, 128).T)
    bv = np.ascontiguousarray(np.broadcast_to(
        np.asarray(inputs["L0_attn_bv"], f32).reshape(1, D), (128, D)))
    bo = np.ascontiguousarray(np.broadcast_to(
        np.asarray(inputs["L0_attn_bo"], f32).reshape(1, D), (128, D)))
    cw = np.asarray(inputs["L0_ffn_conv_w"], f32).reshape(DH, 3)
    cw_l = np.ascontiguousarray(
        cw.reshape(HC, 128, 3).transpose(1, 0, 2).reshape(128, 3 * HC))
    cb_l = np.ascontiguousarray(
        np.asarray(inputs["L0_ffn_conv_b"], f32).reshape(HC, 128).T)
    eye16 = np.eye(128, dtype=ml_dtypes.bfloat16)
    eye32 = np.eye(128, dtype=f32)
    return {
        "wq_t": wq, "wk_t": wk, "wv_t": wv, "wo_t": wo, "w1_t": w1, "w2_t": w2,
        "bq_l": bq, "bk_l": bk, "bv_b": bv, "bo_b": bo,
        "cw_l": cw_l, "cb_l": cb_l, "eye16": eye16, "eye32": eye32,
    }


def kernel(**inputs):
    if "nc" not in _CACHE:
        _CACHE["nc"] = build_nc()
    nc = _CACHE["nc"]
    shared = _host_inputs(inputs)
    x = np.asarray(inputs["L0_attn_x"], np.float32)
    in_maps = []
    for c in range(NCORES):
        m = dict(shared)
        m["x"] = np.ascontiguousarray(x[c * SEQS:(c + 1) * SEQS])
        in_maps.append(m)
    res = run_bass_kernel_spmd(nc, in_maps, core_ids=list(range(NCORES)))
    return np.concatenate([r["out"] for r in res.results], axis=0)


# version 15
# speedup vs baseline: 1.0354x; 1.0354x over previous
import numpy as np
import ml_dtypes
import concourse.bass as bass
import concourse.mybir as mybir
from concourse.bass_utils import run_bass_kernel_spmd

F32 = mybir.dt.float32
BF16 = mybir.dt.bfloat16
I32 = mybir.dt.int32
AF = mybir.ActivationFunctionType
ALU = mybir.AluOpType
AX = mybir.AxisListType

NCORES = 8
SEQS = 2
S = 2048
D = 1024
H = 16
E = 64
DH = 4096
NT = S // 128
DC = D // 128
HC = DH // 128
VW = H * (E + 1)

ENGS = ("sync", "gpsimd", "tensor", "vector", "scalar")
USE_LNEXP_IV = False

_CACHE = {}


class Sem:
    def __init__(self, nc, name):
        self.h = nc.alloc_semaphore(name)
        self.n = 0


class Em:

    def __init__(self, nc, sems):
        self.nc = nc
        self.sems = sems
        self.reset()

    def reset(self):
        self.streams = {e: [] for e in ENGS}
        self.marks = {e: {} for e in ENGS}

    def emit(self, eng, fn, inc=None, k=1, waits=()):
        if eng == "vector":
            waits = list(waits) + [(self.sems["dv"], None)]
            if inc is None:
                inc, k = self.sems["dv"], 1
        elif eng == "scalar":
            waits = list(waits) + [(self.sems["ac"], None)]
            if inc is None:
                inc, k = self.sems["ac"], 1
        wl = []
        m = self.marks[eng]
        for sem, val in waits:
            if val is None:
                val = sem.n
            if val <= 0:
                continue
            if m.get(id(sem), -1) >= val:
                continue
            m[id(sem)] = val
            wl.append((sem.h, val))
        if inc is not None:
            inc.n += k

        def play(e, fn=fn, wl=wl, inc=inc, k=k):
            for h, v in wl:
                e.wait_ge(h, v)
            ins = fn(e)
            if inc is not None and ins is not None:
                ins.then_inc(inc.h, k)
        self.streams[eng].append(play)
        return inc.n if inc is not None else None

    def play(self):
        with self.nc.Block() as blk:
            for name in ENGS:
                fns = self.streams[name]
                if not fns:
                    continue

                def body(e, fns=fns):
                    for f in fns:
                        f(e)
                getattr(blk, name)(body)
        self.reset()


def rearr(t, spec, **kw):
    return t[:].rearrange(spec, **kw)


def rearr_ap3(t_or_ap):
    return rearr(t_or_ap, "p (c t) -> p c t", c=4)


def build_nc():
    nc = bass.Bass("TRN2", target_bir_lowering=False)

    dr = {}
    dr["x"] = nc.dram_tensor("x", [SEQS, S, D], F32, kind="ExternalInput")
    for n, shp in [("wq_t", [D, D]), ("wk_t", [D, D]), ("wv_t", [D, D]),
                   ("wo_t", [D, D]), ("w1_t", [D, DH]), ("w2_t", [DH, D]),
                   ("bq_l", [128, DC]), ("bk_l", [128, DC]),
                   ("bv_b", [128, D]), ("bo_b", [128, D]),
                   ("cw_l", [128, 3 * HC]), ("cb_l", [128, HC]),
                   ("eye32", [128, 128])]:
        dr[n] = nc.dram_tensor(n, shp, F32, kind="ExternalInput")
    dr["eye16"] = nc.dram_tensor("eye16", [128, 128], BF16, kind="ExternalInput")
    dr["out"] = nc.dram_tensor("out", [SEQS, S, D], F32, kind="ExternalOutput")
    dr["x2scr"] = nc.dram_tensor("x2scr", [SEQS, S, D], F32, kind="Internal")
    dr["fscr"] = nc.dram_tensor("fscr", [S, D], F32, kind="Internal")
    dr["w1qscr"] = nc.dram_tensor("w1qscr", [128, DC * DH], BF16, kind="Internal")
    dr["w2qscr"] = nc.dram_tensor("w2qscr", [128, HC * D], BF16, kind="Internal")

    sb = {}
    sb["ones32"] = nc.alloc_sbuf_tensor("ones32", [128, 128], F32)
    sb["ones16"] = nc.alloc_sbuf_tensor("ones16", [128, 128], BF16)
    sb["eye16"] = nc.alloc_sbuf_tensor("eye16s", [128, 128], BF16)
    sb["eye32"] = nc.alloc_sbuf_tensor("eye32s", [128, 128], F32)
    sb["bq"] = nc.alloc_sbuf_tensor("bq_sb", [128, DC], F32)
    sb["bk"] = nc.alloc_sbuf_tensor("bk_sb", [128, DC], F32)
    sb["bv"] = nc.alloc_sbuf_tensor("bv_bc", [128, D], BF16)
    sb["bo"] = nc.alloc_sbuf_tensor("bo_bc", [128, D], BF16)
    sb["cw"] = nc.alloc_sbuf_tensor("cw_sb", [128, 3 * HC], F32)
    sb["cb"] = nc.alloc_sbuf_tensor("cb_sb", [128, HC], F32)
    sb["swcol"] = nc.alloc_sbuf_tensor("swcol", [128, 2], F32)
    sb["iwcol"] = nc.alloc_sbuf_tensor("iwcol", [128, 2], F32)
    sb["invsx16"] = nc.alloc_sbuf_tensor("invsx16", [128, NT], F32)
    sb["st6"] = nc.alloc_sbuf_tensor("st6", [128, 12], F32)
    sb["st2"] = nc.alloc_sbuf_tensor("st2", [128, 2], F32)
    sb["sdc"] = nc.alloc_sbuf_tensor("sdc", [128, 8], F32)
    sb["acc2"] = nc.alloc_sbuf_tensor("acc2", [128, 2], F32)
    sb["t1c"] = nc.alloc_sbuf_tensor("t1c", [128, 2], F32)
    sb["srow"] = nc.alloc_sbuf_tensor("srow", [1, 4], F32)
    sb["epsc"] = nc.alloc_sbuf_tensor("epsc", [128, 1], F32)
    nc.const_aps.register(F32, 1e-5, sb["epsc"][:])
    sb["ivrow"] = nc.alloc_sbuf_tensor("ivrow", [1, 128], F32)

    sems = {k: Sem(nc, k) for k in
            ("scst", "gcst", "sw0", "sw1", "sx0", "sx1", "gw0", "gw1",
             "sqa", "ska", "sod", "ss0", "ss1", "sws", "sf",
             "srf0", "srf1", "srx0", "srx1", "so0", "so1",
             "pe", "dv", "ac", "gc")}
    em = Em(nc, sems)
    ctx = {"nc": nc, "dr": dr, "sb": sb, "em": em, **sems}

    ctx["use_lnexp_iv"] = USE_LNEXP_IV
    _p0(ctx)
    for s in range(SEQS):
        _seq(ctx, s)

    with nc.Block() as blk:
        @blk.sync
        def _(sync):
            sync.wait_ge(sems["so0"].h, sems["so0"].n)
            sync.wait_ge(sems["so1"].h, sems["so1"].n)
    return nc


def _p0(ctx):
    nc, dr, sb, em = ctx["nc"], ctx["dr"], ctx["sb"], ctx["em"]
    pe, dv, gc = ctx["pe"], ctx["dv"], ctx["gc"]
    scst, gcst = ctx["scst"], ctx["gcst"]
    sw = (ctx["sw0"], ctx["sw1"])

    for name, dst in [("eye16", "eye16"), ("eye32", "eye32"),
                      ("bq_l", "bq"), ("bk_l", "bk"),
                      ("cw_l", "cw"), ("cb_l", "cb")]:
        em.emit("sync", lambda e, a=dr[name], b=sb[dst]: e.dma_start(b[:], a[:]),
                inc=scst, k=16)
    for name, dst in [("bv_b", "bv"), ("bo_b", "bo")]:
        em.emit("gpsimd", lambda e, a=dr[name], b=sb[dst]: e.dma_start(b[:], a[:]),
                inc=gcst, k=16)
    em.emit("gpsimd", lambda e: e.memset(sb["ones32"][:], 1.0))
    em.emit("gpsimd", lambda e: e.memset(sb["ones16"][:], 1.0))
    em.emit("gpsimd", lambda e: e.memset(sb["epsc"][:], 1e-5))
    em.emit("gpsimd", lambda e: e.memset(sb["acc2"][:], 0.0), inc=gc)
    em.play()

    with (
        nc.sbuf_tensor("p0st", [128, 2 * DH], F32) as p0st,
        nc.psum_tensor("p0ps", [128, 8], F32) as p0ps,
    ):
        chunks = [("w1_t", c * 128, DH, 0) for c in range(DC)] + \
                 [("w2_t", c * 128, D, 1) for c in range(HC)]
        red_done = []
        for j, (src, r0, ncol, col) in enumerate(chunks):
            w = [(dv, red_done[j - 2])] if j >= 2 else []
            em.emit("sync",
                    lambda e, src=dr[src], r0=r0, ncol=ncol, j=j:
                    e.dma_start(p0st[:, (j % 2) * DH:(j % 2) * DH + ncol],
                                src[r0:r0 + 128, :]),
                    inc=sw[j % 2], k=16, waits=w)
            em.emit("vector",
                    lambda e, ncol=ncol, j=j, col=col:
                    e.reduce_sum(sb["t1c"][:, col:col + 1],
                                 p0st[:, (j % 2) * DH:(j % 2) * DH + ncol],
                                 AX.X, apply_absolute_value=True),
                    waits=[(sw[j % 2], None)])
            em.emit("vector",
                    lambda e, col=col:
                    e.tensor_add(sb["acc2"][:, col:col + 1],
                                 sb["acc2"][:, col:col + 1],
                                 sb["t1c"][:, col:col + 1]),
                    waits=[(gc, None)])
            red_done.append(dv.n)
        em.emit("tensor", lambda e: e.matmul(p0ps[0:1, 0:2], sb["ones32"][:, 0:1],
                                             sb["acc2"][:], start=True, stop=True),
                inc=pe, waits=[(dv, None)])
        em.emit("vector", lambda e: e.tensor_scalar(
            sb["srow"][0:1, 0:2], p0ps[0:1, 0:2], 1.0 / (D * DH), 1e-5,
            ALU.mult, ALU.max), waits=[(pe, None)])
        em.emit("vector", lambda e: e.reciprocal(
            sb["srow"][0:1, 2:4], sb["srow"][0:1, 0:2]))
        em.emit("tensor", lambda e: e.matmul(p0ps[:, 2:4], sb["ones32"][0:1, :],
                                             sb["srow"][0:1, 2:4],
                                             start=True, stop=True),
                inc=pe, waits=[(dv, None)])
        em.emit("tensor", lambda e: e.matmul(p0ps[:, 4:6], sb["ones32"][0:1, :],
                                             sb["srow"][0:1, 0:2],
                                             start=True, stop=True), inc=pe)
        em.emit("vector", lambda e: e.tensor_copy(sb["swcol"][:], p0ps[:, 2:4]),
                waits=[(pe, None)])
        em.emit("vector", lambda e: e.tensor_copy(sb["iwcol"][:], p0ps[:, 4:6]))
        em.play()


def _ln_stats(ctx, src_ap, w=()):
    em, sb = ctx["em"], ctx["sb"]
    em.emit("vector", lambda e: e.bn_stats(sb["st6"][:, 0:6], src_ap[:, 0:512]),
            waits=list(w))
    em.emit("vector", lambda e: e.bn_stats(sb["st6"][:, 6:12],
                                           src_ap[:, 512:1024]))
    em.emit("vector", lambda e: e.bn_aggr(sb["st2"][:], sb["st6"][:]))


def _ln_rsig(ctx):
    em, dv, sb = ctx["em"], ctx["dv"], ctx["sb"]
    em.emit("scalar", lambda e: e.activation(sb["sdc"][:, 0:1], sb["st2"][:, 1:2],
                                             AF.Sqrt, bias=1e-5),
            waits=[(dv, None)])
    em.emit("vector", lambda e: e.reciprocal(sb["sdc"][:, 1:2], sb["sdc"][:, 0:1]),
            waits=[(ctx["ac"], None)])


def _round_to_bf16(ctx, y_ap, out_ap, scr, outw=()):
    em = ctx["em"]
    em.emit("vector", lambda e: e.tensor_copy(scr["i32"], y_ap))
    em.emit("vector", lambda e: e.tensor_scalar(scr["r0"], scr["i32"], -1.0,
                                                None, ALU.add))
    em.emit("vector", lambda e: e.tensor_sub(scr["d"], y_ap, scr["r0"]))
    em.emit("vector", lambda e: e.scalar_tensor_tensor(scr["f1"], scr["d"], 1.5,
                                                       scr["r0"], ALU.is_gt,
                                                       ALU.add))
    em.emit("vector", lambda e: e.scalar_tensor_tensor(out_ap, scr["d"], 0.5,
                                                       scr["f1"], ALU.is_ge,
                                                       ALU.add),
            waits=list(outw))


def _seq(ctx, s):
    nc = ctx["nc"]
    with (
        nc.sbuf_tensor(f"s{s}_xqfm", [128, DC * S], BF16) as xqfm,
        nc.sbuf_tensor(f"s{s}_ivfm", [128, S], F32) as ivfm,
    ):
        with nc.sbuf_tensor(f"s{s}_big32", [128, DC * S], BF16) as big32:
            with (
                nc.sbuf_tensor(f"s{s}_qpk", [128, DC * S], BF16) as qpk,
                nc.sbuf_tensor(f"s{s}_kpk", [128, DC * S], BF16) as kpk,
                nc.sbuf_tensor(f"s{s}_vsb", [128, NT * VW], BF16) as vsb,
            ):
                _p1_ln1(ctx, s, big32)
                _p2_qkv(ctx, s, big32, qpk, kpk, vsb, (xqfm, [0, DC * D]))
                with (
                    nc.sbuf_tensor(f"s{s}_pt", [128, 3 * S], BF16) as pt,
                    nc.sbuf_tensor(f"s{s}_oddh", [64, S], BF16) as oddh,
                    nc.sbuf_tensor(f"s{s}_qa", [65, 2 * S], BF16) as qa,
                    nc.sbuf_tensor(f"s{s}_ka", [65, 2 * S], BF16) as ka,
                    nc.sbuf_tensor(f"s{s}_mke", [64, 65], BF16) as mke,
                ):
                    ctx["ivb16"] = xqfm
                    ctx["absq"] = oddh
                    _p3_attn(ctx, s, qpk, kpk, vsb, big32, pt, ivfm, oddh,
                             qa, ka, mke)
            _p4_proj_ln(ctx, s, big32, xqfm, ivfm)
        with (
            nc.sbuf_tensor(f"s{s}_w1q", [128, DC * DH], BF16) as w1q,
            nc.sbuf_tensor(f"s{s}_w2q", [128, HC * D], BF16) as w2q,
        ):
            _p45_wquant(ctx, s, w1q, w2q)
            _p5_ffn(ctx, s, xqfm, ivfm, w1q, w2q)


def _p1_ln1(ctx, s, xn1):
    nc, dr, sb, em = ctx["nc"], ctx["dr"], ctx["sb"], ctx["em"]
    pe, dv = ctx["pe"], ctx["dv"]
    sx = (ctx["sx0"], ctx["sx1"])
    with (
        nc.sbuf_tensor(f"s{s}_xint", [128, 2 * D], F32) as xint,
        nc.sbuf_tensor(f"s{s}_xn1t", [128, D], BF16) as xn1t,
        nc.psum_tensor(f"s{s}_p1ps", [128, D], BF16) as p1ps,
    ):
        consumed = []
        for i in range(NT):
            xb = xint[:, (i % 2) * D:(i % 2) * D + D]
            w = [(dv, consumed[i - 2])] if i >= 2 else []
            if i == 0:
                w += [(ctx["so0"], None), (ctx["so1"], None),
                      (ctx["sws"], None), (ctx["sf"], None)]
            em.emit("sync", lambda e, i=i, xb=xb: e.dma_start(
                xb, dr["x"][s, i * 128:(i + 1) * 128, :]),
                inc=sx[i % 2], k=16, waits=w)
            _ln_stats(ctx, xb, w=[(sx[i % 2], None), (ctx["scst"], None),
                                  (ctx["gcst"], None)])
            _ln_rsig(ctx)
            em.emit("vector", lambda e, xb=xb: e.tensor_scalar(
                xn1t[:], xb, sb["st2"][:, 0:1], sb["sdc"][:, 1:2],
                ALU.subtract, ALU.mult), waits=[(pe, None)])
            consumed.append(ctx["dv"].n)
            for c in range(DC):
                em.emit("tensor", lambda e, c=c: e.transpose(
                    p1ps[:, c * 128:(c + 1) * 128],
                    xn1t[:, c * 128:(c + 1) * 128], sb["eye16"][:]),
                    inc=pe if c == DC - 1 else None,
                    waits=[(dv, None)] if c == 0 else [])
            em.emit("vector", lambda e, i=i: e.tensor_copy(
                rearr(xn1, "p (c t) -> p c t", c=DC)[:, :, i * 128:(i + 1) * 128],
                rearr(p1ps, "p (c t) -> p c t", c=DC)),
                waits=[(pe, None)])
        em.play()


def _p2_qkv(ctx, s, xn1, qpk, kpk, vsb, wbufs):
    nc, dr, sb, em = ctx["nc"], ctx["dr"], ctx["sb"], ctx["em"]
    pe, dv, gc = ctx["pe"], ctx["dv"], ctx["gc"]
    gw = (ctx["gw0"], ctx["gw1"])
    wtens, wofs = wbufs

    def wload(idx, name, waits=()):
        src = rearr(dr[name], "(dc p) he -> p dc he", p=128)
        em.emit("gpsimd", lambda e, src=src, idx=idx: e.dma_start(
            rearr(wtens, "p (g dc he) -> p g dc he", g=2, dc=DC)[:, idx],
            src),
            inc=gw[idx], k=16, waits=list(waits))
        return gw[idx].n

    with (
        nc.psum_tensor(f"s{s}_qps0", [128, S], F32) as qps0,
        nc.psum_tensor(f"s{s}_qps1", [128, S], F32) as qps1,
    ):
        qps = (qps0, qps1)
        gwq = wload(0, "wq_t", waits=[(ctx["so0"], None), (ctx["so1"], None),
                                       (ctx["sws"], None), (ctx["sf"], None)])
        gwk = wload(1, "wk_t")
        gwv = [None]
        evac_done = {}
        for proj, (dst, bias, scal, wb, gwn) in enumerate(
                [(qpk, "bq", 0.125, 0, gwq), (kpk, "bk", None, 1, gwk)]):
            for oc in range(DC):
                ps = qps[oc % 2]
                w = [(gw[wb], gwn)]
                prev = (proj, oc - 2)
                if oc < 2:
                    prev = (proj - 1, oc + DC - 2)
                if prev in evac_done:
                    w.append((dv, evac_done[prev]))
                for dc in range(DC):
                    for nb in range(4):
                        em.emit("tensor", lambda e, ps=ps, oc=oc, dc=dc, nb=nb,
                                wo=wofs[wb]: e.matmul(
                            ps[:, nb * 512:(nb + 1) * 512],
                            wtens[:, wo + dc * D + oc * 128:wo + dc * D + oc * 128 + 128],
                            xn1[:, dc * S + nb * 512:dc * S + (nb + 1) * 512],
                            start=(dc == 0), stop=(dc == DC - 1)),
                            inc=pe if (dc == DC - 1 and nb == 3) else None,
                            waits=w if (dc == 0 and nb == 0) else [])
                if scal is None:
                    em.emit("vector", lambda e, ps=ps, oc=oc, dst=dst, bias=bias:
                            e.tensor_scalar(dst[:, oc * S:(oc + 1) * S], ps[:],
                                            sb[bias][:, oc:oc + 1], None,
                                            ALU.add),
                            waits=[(pe, None)])
                else:
                    em.emit("vector", lambda e, ps=ps, oc=oc, dst=dst, bias=bias,
                            sc=scal: e.tensor_scalar(
                        dst[:, oc * S:(oc + 1) * S], ps[:],
                        sb[bias][:, oc:oc + 1], sc, ALU.add, ALU.mult),
                        waits=[(pe, None)])
                evac_done[(proj, oc)] = dv.n
            if proj == 0:
                gwv[0] = wload(0, "wv_t", waits=[(pe, None)])
        em.play()

    for kt in range(NT):
        em.emit("gpsimd", lambda e, kt=kt: e.memset(
            rearr(vsb, "p (kt h e) -> p kt h e", kt=NT, h=H)[:, kt, :, 64:65],
            1.0), inc=gc if kt == NT - 1 else None)
    with (
        nc.psum_tensor(f"s{s}_vps0", [128, D], F32) as vps0,
        nc.psum_tensor(f"s{s}_vps1", [128, D], F32) as vps1,
    ):
        vps = (vps0, vps1)
        evd = []
        for tt in range(NT):
            ps = vps[tt % 2]
            w = [(ctx["gw0"], gwv[0])]
            if tt >= 2:
                w.append((dv, evd[tt - 2]))
            for dc in range(DC):
                for nb in range(2):
                    em.emit("tensor", lambda e, ps=ps, tt=tt, dc=dc, nb=nb:
                            e.matmul(
                        ps[:, nb * 512:(nb + 1) * 512],
                        xn1[:, dc * S + tt * 128:dc * S + tt * 128 + 128],
                        wtens[:, wofs[0] + dc * D + nb * 512:wofs[0] + dc * D + (nb + 1) * 512],
                        start=(dc == 0), stop=(dc == DC - 1)),
                        inc=pe if (dc == DC - 1 and nb == 1) else None,
                        waits=w if (dc == 0 and nb == 0) else [])
            em.emit("vector", lambda e, ps=ps, tt=tt: e.tensor_tensor(
                rearr(vsb, "p (kt h e) -> p kt h e", kt=NT, h=H)[:, tt, :, 0:64],
                rearr(ps, "p (h e) -> p h e", h=H),
                rearr(sb["bv"], "p (h e) -> p h e", h=H), ALU.add),
                waits=[(pe, None), (gc, None)])
            evd.append(dv.n)
        em.play()


def _p3_attn(ctx, s, qpk, kpk, vsb, attn, pt, ivr, oddh, qa, ka, mke):
    nc, sb, em = ctx["nc"], ctx["sb"], ctx["em"]
    pe, dv, ac, gc = ctx["pe"], ctx["dv"], ctx["ac"], ctx["gc"]
    sqa, ska, sod = ctx["sqa"], ctx["ska"], ctx["sod"]
    absq = ctx["absq"]
    Q = 1024

    with (
        nc.psum_tensor(f"s{s}_sc0", [128, Q], F32) as sc0,
        nc.psum_tensor(f"s{s}_sc1", [128, Q], F32) as sc1,
        nc.psum_tensor(f"s{s}_atps", [65, S], F32) as atps,
    ):
        scp = (sc0, sc1)
        em.emit("gpsimd", lambda e: e.memset(ka[64:65, :], 1.0))
        em.emit("gpsimd", lambda e: e.memset(mke[:], 0.0), inc=gc)

        odd_dma = {}
        attv_last = {}
        attv_cnt = {}
        exp_cnt = {}
        mult_done = {}
        sc_cnt = {}

        def aug_prep(h):
            if h >= H:
                return
            c, half, b = h // 2, h % 2, h % 2
            qsl = slice(c * S, (c + 1) * S)
            wbuf = []
            if h >= 2:
                wbuf.append((pe, attv_last[h - 2]))
            for src, dstt, dsem in ((qpk, qa, sqa), (kpk, ka, ska)):
                if half == 0:
                    em.emit("vector", lambda e, src=src, dstt=dstt, qsl=qsl, b=b:
                            e.tensor_copy(dstt[0:64, b * S:(b + 1) * S],
                                          src[0:64, qsl]),
                            waits=wbuf)
                else:
                    em.emit("sync", lambda e, src=src, dstt=dstt, qsl=qsl, b=b:
                            e.dma_start(dstt[0:64, b * S:(b + 1) * S],
                                        src[64:128, qsl]),
                            inc=dsem, k=16, waits=wbuf)
            em.emit("vector", lambda e, b=b: e.reduce_max(
                mke[:, 64:65], ka[0:64, b * S:(b + 1) * S], AX.X,
                apply_absolute_value=True),
                waits=[(ska, None), (gc, None), (pe, None)])

        def emit_absq(h):
            if h >= H:
                return
            b = h % 2
            em.emit("vector", lambda e, b=b: e.tensor_scalar_mul(
                absq[0:64, :], qa[0:64, b * S:(b + 1) * S], -1.0),
                waits=[(sqa, None), (sod, None)])
            em.emit("vector", lambda e, b=b: e.tensor_tensor(
                absq[0:64, :], absq[0:64, :], qa[0:64, b * S:(b + 1) * S],
                ALU.max))

        def emit_M(h, j):
            for nb in range(2):
                em.emit("tensor", lambda e, nb=nb, j=j: e.matmul(
                    scp[j][0:65, nb * 512:(nb + 1) * 512], mke[:],
                    absq[0:64, j * Q + nb * 512:j * Q + (nb + 1) * 512],
                    start=True, stop=True),
                    inc=pe if nb == 1 else None,
                    waits=([(dv, None), (ac, exp_cnt.get((h - 1, NT - 1, j), 0))]
                           if nb == 0 else []))
            b = h % 2
            em.emit("vector", lambda e, b=b, j=j: e.tensor_scalar(
                qa[64:65, b * S + j * Q:b * S + (j + 1) * Q],
                scp[j][64:65, :], 80.0, -1.0,
                ALU.min, ALU.mult), waits=[(pe, None)])
            return dv.n

        def emit_scores(h, kt, j, negm):
            b = h % 2
            for nb in range(2):
                w = []
                if nb == 0:
                    if kt == 0:
                        w = [(dv, negm[j])]
                    else:
                        w = [(ac, exp_cnt[(h, kt - 1, j)])]
                    if kt == 7:
                        w.append((dv, mult_done.get(h - 1, 0)))
                em.emit("tensor", lambda e, kt=kt, nb=nb, b=b, j=j: e.matmul(
                    scp[j][:, nb * 512:(nb + 1) * 512],
                    ka[:, b * S + kt * 128:b * S + kt * 128 + 128],
                    qa[:, b * S + j * Q + nb * 512:b * S + j * Q + (nb + 1) * 512],
                    start=True, stop=True),
                    inc=pe if nb == 1 else None, waits=w)
            sc_cnt[(h, kt, j)] = pe.n

        def emit_exp(h, kt, j):
            p = kt % 3
            w = [(pe, sc_cnt[(h, kt, j)])]
            if kt >= 3:
                w.append((pe, attv_cnt.get((h, kt - 3, j), 0)))
            em.emit("scalar", lambda e, p=p, j=j: e.activation(
                pt[:, p * S + j * Q:p * S + (j + 1) * Q], scp[j][:], AF.Exp),
                waits=w)
            exp_cnt[(h, kt, j)] = ac.n

        def emit_attv(h, kt, j):
            p = kt % 3
            for nb in range(2):
                w = []
                if nb == 0:
                    w = [(ac, exp_cnt[(h, kt, j)])]
                    if kt == 0:
                        w.append((dv, ctx.get(f"_cpA{h - 1}", 0)))
                em.emit("tensor", lambda e, kt=kt, nb=nb, h=h, p=p, j=j: e.matmul(
                    atps[0:65, j * Q + nb * 512:j * Q + (nb + 1) * 512],
                    vsb[:, kt * VW + h * 65:kt * VW + (h + 1) * 65],
                    pt[:, p * S + j * Q + nb * 512:p * S + j * Q + (nb + 1) * 512],
                    start=(kt == 0), stop=(kt == NT - 1)),
                    inc=pe if nb == 1 else None, waits=w)
            attv_cnt[(h, kt, j)] = pe.n

        def emit_evac(h):
            em.emit("vector", lambda e: e.tensor_copy(ivr[0:65, :],
                                                      atps[0:65, :]),
                    waits=[(pe, attv_cnt[(h, NT - 1, 1)])])
            ctx[f"_cpA{h}"] = dv.n

        def emit_recip(h):
            if h < 0:
                return
            em.emit("vector", lambda e: e.reciprocal(
                ivr[64:65, :], ivr[64:65, :]))
            em.emit("vector", lambda e: e.tensor_copy(
                ctx["ivb16"][64:65, 0:S], ivr[64:65, :]))
            ctx[f"_iv{h}"] = dv.n

        def emit_post(h):
            if h < 0:
                return
            c, half = h // 2, h % 2
            for j in range(2):
                for nb in range(2):
                    w = []
                    if nb == 0:
                        w = [(dv, ctx[f"_iv{h}"]), (ac, None)]
                    em.emit("tensor", lambda e, nb=nb, j=j: e.matmul(
                        scp[j][0:64, nb * 512:(nb + 1) * 512],
                        sb["ones32"][64:65, 0:64],
                        ivr[64:65, j * Q + nb * 512:j * Q + (nb + 1) * 512],
                        start=True, stop=True),
                        inc=pe if nb == 1 else None, waits=w)
                dst = attn[0:64, c * S + j * Q:c * S + (j + 1) * Q] \
                    if half == 0 else oddh[0:64, j * Q:(j + 1) * Q]
                ww = [(pe, None)]
                if half == 1 and j == 0:
                    ww.append((sod, odd_dma.get(h - 2, 0)))
                em.emit("vector", lambda e, dst=dst, j=j: e.tensor_tensor(
                    dst, ivr[0:64, j * Q:(j + 1) * Q], scp[j][0:64, :],
                    ALU.mult), waits=ww)
            if half == 1:
                em.emit("sync", lambda e, c=c: e.dma_start(
                    attn[64:128, c * S:(c + 1) * S], oddh[0:64, :]),
                    inc=sod, k=16, waits=[(dv, None)])
                odd_dma[h] = sod.n
            mult_done[h] = dv.n

        aug_prep(0)
        emit_absq(0)
        for h in range(H):
            negm = [emit_M(h, 0), emit_M(h, 1)]
            aug_prep(h + 1)
            emit_recip(h - 1)
            for kt in range(3):
                for j in range(2):
                    emit_scores(h, kt, j, negm)
                    emit_exp(h, kt, j)
            emit_attv(h, 0, 0)
            emit_attv(h, 0, 1)
            for kt in range(3, NT):
                if kt == 7:
                    emit_post(h - 1)
                    emit_absq(h + 1)
                for j in range(2):
                    emit_scores(h, kt, j, negm)
                    emit_exp(h, kt, j)
                    emit_attv(h, kt - 2, j)
            for j in range(2):
                emit_attv(h, NT - 2, j)
                emit_attv(h, NT - 1, j)
            attv_last[h] = attv_cnt[(h, NT - 1, 1)]
            emit_evac(h)
        emit_recip(H - 1)
        emit_post(H - 1)
        em.play()
    ctx["attn_sod"] = ctx["sod"].n
def _p4_proj_ln(ctx, s, attn, xqfm, ivfm):
    nc, dr, sb, em = ctx["nc"], ctx["dr"], ctx["sb"], ctx["em"]
    pe, dv, ac = ctx["pe"], ctx["dv"], ctx["ac"]
    sx = (ctx["sx0"], ctx["sx1"])
    ss = (ctx["ss0"], ctx["ss1"])
    gw1 = ctx["gw1"]
    with (
        nc.sbuf_tensor(f"s{s}_wo", [128, DC * D], BF16) as wo,
        nc.sbuf_tensor(f"s{s}_xint2", [128, 2 * D], F32) as xint2,
        nc.sbuf_tensor(f"s{s}_ao", [128, D], F32) as ao,
        nc.sbuf_tensor(f"s{s}_x2t", [128, 2 * D], F32) as x2t,
        nc.sbuf_tensor(f"s{s}_xn2", [128, D], F32) as xn2,
        nc.sbuf_tensor(f"s{s}_yq", [128, D], F32) as yq,
        nc.sbuf_tensor(f"s{s}_qi32", [128, D], I32) as qi32,
        nc.sbuf_tensor(f"s{s}_r0f", [128, D], F32) as r0f,
        nc.sbuf_tensor(f"s{s}_dbf", [128, D], F32) as dbf,
        nc.sbuf_tensor(f"s{s}_f1f", [128, D], F32) as f1f,
        nc.sbuf_tensor(f"s{s}_xqt", [128, D], BF16) as xqt,
        nc.psum_tensor(f"s{s}_op0", [128, D], F32) as op0,
        nc.psum_tensor(f"s{s}_op1", [128, D], F32) as op1,
        nc.psum_tensor(f"s{s}_trp", [128, D], BF16) as trp,
        nc.psum_tensor(f"s{s}_ivt", [1, 128], F32) as ivt,
        nc.psum_tensor(f"s{s}_ivb", [128, 128], F32) as ivb,
    ):
        em.emit("gpsimd", lambda e: e.dma_start(
            rearr(wo, "p (c d) -> p c d", c=DC),
            rearr(dr["wo_t"], "(c p) d -> p c d", p=128)), inc=gw1, k=16,
            waits=[(ctx["sod"], ctx["attn_sod"]), (ctx["sqa"], None),
                   (ctx["ska"], None)])
        scr = {"i32": qi32[:], "r0": r0f[:], "d": dbf[:], "f1": f1f[:]}
        ops = (op0, op1)
        x_used = []
        evac_done = []
        spill_sp = []
        pend_spill = []
        for i in range(NT):
            xb = xint2[:, (i % 2) * D:(i % 2) * D + D]
            w = [(dv, x_used[i - 2])] if i >= 2 else []
            em.emit("sync", lambda e, i=i, xb=xb: e.dma_start(
                xb, dr["x"][s, i * 128:(i + 1) * 128, :]),
                inc=sx[i % 2], k=16, waits=w)
            ps = ops[i % 2]
            w = [(gw1, None), (ctx["sod"], ctx["attn_sod"])]
            if i >= 2:
                w.append((dv, evac_done[i - 2]))
            for c in range(DC):
                for nb in range(2):
                    em.emit("tensor", lambda e, ps=ps, i=i, c=c, nb=nb: e.matmul(
                        ps[:, nb * 512:(nb + 1) * 512],
                        attn[:, c * S + i * 128:c * S + i * 128 + 128],
                        wo[:, c * D + nb * 512:c * D + (nb + 1) * 512],
                        start=(c == 0), stop=(c == DC - 1)),
                        inc=pe if (c == DC - 1 and nb == 1) else None,
                        waits=w if (c == 0 and nb == 0) else [])
            em.emit("vector", lambda e, ps=ps: e.tensor_tensor(
                ao[:], ps[:], sb["bo"][:], ALU.add), waits=[(pe, None)])
            evac_done.append(dv.n)
            _ln_stats(ctx, ao[:])
            _ln_rsig(ctx)
            em.emit("vector", lambda e: e.tensor_scalar(
                yq[:], ao[:], sb["st2"][:, 0:1], sb["sdc"][:, 1:2],
                ALU.subtract, ALU.mult))
            x2b = x2t[:, (i % 2) * D:(i % 2) * D + D]
            xw = [(sx[i % 2], None)]
            if i >= 2:
                xw.append((ss[i % 2], spill_sp[i - 2]))
            em.emit("vector", lambda e, xb=xb, x2b=x2b: e.tensor_add(
                x2b, yq[:], xb), waits=xw)
            x_used.append(dv.n)
            pend_spill.append((i, x2b, dv.n))
            if len(pend_spill) > 1 or i == NT - 1:
                for (pi, px, pdv) in pend_spill[:None if i == NT - 1 else -1]:
                    em.emit("sync", lambda e, pi=pi, px=px: e.dma_start(
                        dr["x2scr"][s, pi * 128:(pi + 1) * 128, :], px),
                        inc=ss[pi % 2], k=16, waits=[(dv, pdv)])
                    spill_sp.append(ss[pi % 2].n)
                pend_spill[:] = ([] if i == NT - 1 else pend_spill[-1:])
            _ln_stats(ctx, x2b)
            _ln_rsig(ctx)
            em.emit("vector", lambda e, x2b=x2b: e.tensor_scalar(
                xn2[:], x2b, sb["st2"][:, 0:1], sb["sdc"][:, 1:2],
                ALU.subtract, ALU.mult))
            em.emit("vector", lambda e: e.reduce_max(
                sb["sdc"][:, 2:3], xn2[:], AX.X, apply_absolute_value=True))
            em.emit("vector", lambda e: e.tensor_scalar_max(
                sb["sdc"][:, 3:4], sb["sdc"][:, 2:3], 1e-5))
            em.emit("vector", lambda e: e.reciprocal(
                sb["sdc"][:, 4:5], sb["sdc"][:, 3:4]))
            em.emit("vector", lambda e: e.tensor_scalar_mul(
                sb["sdc"][:, 5:6], sb["sdc"][:, 4:5], 127.0))
            em.emit("vector", lambda e, i=i: e.tensor_scalar_mul(
                sb["invsx16"][:, i:i + 1], sb["sdc"][:, 3:4], 1.0 / 127.0))
            em.emit("vector", lambda e: e.tensor_scalar_mul(
                yq[:], xn2[:], sb["sdc"][:, 5:6]))
            _round_to_bf16(ctx, yq[:], xqt[:], scr, outw=[(pe, None)])
            for c in range(DC):
                em.emit("tensor", lambda e, c=c: e.transpose(
                    trp[:, c * 128:(c + 1) * 128], xqt[:, c * 128:(c + 1) * 128],
                    sb["eye16"][:]), inc=pe if c == DC - 1 else None,
                    waits=[(dv, None)] if c == 0 else [])
            em.emit("vector", lambda e, i=i: e.tensor_copy(
                rearr(xqfm, "p (c t) -> p c t", c=DC)[:, :, i * 128:(i + 1) * 128],
                rearr(trp, "p (c t) -> p c t", c=DC)), waits=[(pe, None)])
            em.emit("tensor", lambda e, i=i: e.transpose(
                ivt[0:1, :], sb["invsx16"][:, i:i + 1], sb["eye32"][:]),
                inc=pe, waits=[(dv, None)])
            em.emit("vector", lambda e: e.tensor_copy(sb["ivrow"][0:1, :],
                                                      ivt[0:1, :]),
                    waits=[(pe, None)])
            em.emit("tensor", lambda e: e.matmul(
                ivb[:], sb["ones32"][0:1, :], sb["ivrow"][0:1, :],
                start=True, stop=True), inc=pe, waits=[(dv, None)])
            em.emit("vector", lambda e, i=i: e.tensor_copy(
                ivfm[:, i * 128:(i + 1) * 128], ivb[:]), waits=[(pe, None)])
        em.play()


def _p45_wquant(ctx, s, w1q, w2q):
    nc, dr, sb, em = ctx["nc"], ctx["dr"], ctx["sb"], ctx["em"]
    dv, sws = ctx["dv"], ctx["sws"]
    sw = (ctx["sw0"], ctx["sw1"])
    if s == 1:
        em.emit("sync", lambda e: e.dma_start(w1q[:], dr["w1qscr"][:]),
                inc=sws, k=16,
                waits=[(sws, 32), (ctx["ss0"], None), (ctx["ss1"], None),
                       (ctx["sx0"], None), (ctx["sx1"], None)])
        em.emit("sync", lambda e: e.dma_start(w2q[:], dr["w2qscr"][:]),
                inc=sws, k=16)
        em.play()
        ctx["wq_rdy"] = sws.n
        return
    with (
        nc.sbuf_tensor(f"s{s}_wst", [128, 2 * 2048], F32) as wst,
        nc.sbuf_tensor(f"s{s}_c1b", [128, 2048], BF16) as c1b,
        nc.sbuf_tensor(f"s{s}_c2b", [128, 2048], BF16) as c2b,
    ):
        jobs = []
        for c in range(DC):
            for hcol in range(2):
                jobs.append(("w1_t", c * 128, hcol * 2048, 2048, 0,
                             w1q, c * DH + hcol * 2048))
        for c in range(HC):
            jobs.append(("w2_t", c * 128, 0, D, 1, w2q, c * D))
        qdone = []
        for j, (src, r0, c0, ncol, scol, dst, dcol) in enumerate(jobs):
            stg = wst[:, (j % 2) * 2048:(j % 2) * 2048 + ncol]
            w = [(dv, qdone[j - 2])] if j >= 2 else []
            if j == 0:
                w += [(ctx["ss0"], None), (ctx["ss1"], None),
                      (ctx["sx0"], None), (ctx["sx1"], None)]
            em.emit("sync", lambda e, stg=stg, src=src, r0=r0, c0=c0, ncol=ncol:
                    e.dma_start(stg, dr[src][r0:r0 + 128, c0:c0 + ncol]),
                    inc=sw[j % 2], k=16, waits=w)
            em.emit("vector", lambda e, stg=stg, ncol=ncol, scol=scol:
                    e.tensor_scalar(c1b[:, 0:ncol], stg,
                                    sb["swcol"][:, scol:scol + 1], 0.5,
                                    ALU.mult, ALU.is_ge),
                    waits=[(sw[j % 2], None)])
            em.emit("vector", lambda e, stg=stg, ncol=ncol, scol=scol:
                    e.tensor_scalar(c2b[:, 0:ncol], stg,
                                    sb["swcol"][:, scol:scol + 1], -0.5,
                                    ALU.mult, ALU.is_gt))
            em.emit("vector", lambda e, dst=dst, dcol=dcol, ncol=ncol:
                    e.scalar_tensor_tensor(dst[:, dcol:dcol + ncol],
                                           c1b[:, 0:ncol], -1.0,
                                           c2b[:, 0:ncol], ALU.add, ALU.add))
            qdone.append(dv.n)
        em.emit("sync", lambda e: e.dma_start(dr["w1qscr"][:], w1q[:]),
                inc=sws, k=16, waits=[(dv, None)])
        em.emit("sync", lambda e: e.dma_start(dr["w2qscr"][:], w2q[:]),
                inc=sws, k=16)
        em.play()
        ctx["wq_rdy"] = None


def _p5_ffn(ctx, s, xqfm, ivfm, w1q, w2q):
    nc, dr, sb, em = ctx["nc"], ctx["dr"], ctx["sb"], ctx["em"]
    pe, dv, ac = ctx["pe"], ctx["dv"], ctx["ac"]
    sf, sws = ctx["sf"], ctx["sws"]
    fspill = []
    with (
        nc.sbuf_tensor(f"s{s}_hgr", [128, 2 * 4 * 130], BF16) as hgr,
        nc.sbuf_tensor(f"s{s}_tmpg", [128, 512], BF16) as tmpg,
        nc.sbuf_tensor(f"s{s}_tcv", [128, HC * 128], BF16) as tcv,
        nc.sbuf_tensor(f"s{s}_gq", [128, 2 * 512], BF16) as gqb,
        nc.sbuf_tensor(f"s{s}_gmax", [128, 128], F32) as gmax,
        nc.sbuf_tensor(f"s{s}_sgb", [128, 128], F32) as sgb,
        nc.sbuf_tensor(f"s{s}_yb", [128, 512], F32) as yb,
        nc.sbuf_tensor(f"s{s}_qi2", [128, 512], I32) as qi2,
        nc.sbuf_tensor(f"s{s}_r02", [128, 512], F32) as r02,
        nc.sbuf_tensor(f"s{s}_db2", [128, 512], F32) as db2,
        nc.sbuf_tensor(f"s{s}_f12", [128, 512], F32) as f12,
        nc.sbuf_tensor(f"s{s}_ft", [128, D], F32) as ft,
        nc.psum_tensor(f"s{s}_hps0", [128, 130], F32) as hps0,
        nc.psum_tensor(f"s{s}_hps1", [128, 130], F32) as hps1,
        nc.psum_tensor(f"s{s}_fp0", [128, D], F32) as fp0,
        nc.psum_tensor(f"s{s}_fp1", [128, D], F32) as fp1,
        nc.psum_tensor(f"s{s}_tmp", [128, 128], F32) as tmps,
    ):
        hps = (hps0, hps1)
        fps = (fp0, fp1)
        scr = {"i32": qi2[:], "r0": r02[:], "d": db2[:], "f1": f12[:]}
        fevac = []
        hevac = []
        convd = []
        w2done = {}
        for i in range(NT):
            j0 = 1 if i == 0 else 0
            j1 = 129 if i == NT - 1 else 130
            t0 = i * 128 - 1
            for cc in range(HC):
                idx = i * HC + cc
                g, c4 = cc // 4, cc % 4
                ps = hps[cc % 2]
                if idx >= 2:
                    w = [(dv, hevac[idx - 2])]
                else:
                    w = [(dv, None)]
                    if ctx.get("wq_rdy"):
                        w.append((sws, ctx["wq_rdy"]))
                for dc in range(DC):
                    em.emit("tensor", lambda e, ps=ps, cc=cc, dc=dc,
                            j0=j0, j1=j1, t0=t0: e.matmul(
                        ps[:, j0:j1],
                        w1q[:, dc * DH + cc * 128:dc * DH + cc * 128 + 128],
                        xqfm[:, dc * S + t0 + j0:dc * S + t0 + j1],
                        start=(dc == 0), stop=(dc == DC - 1)),
                        inc=pe if dc == DC - 1 else None,
                        waits=w if dc == 0 else [])
                gb = (g % 2) * 520
                hw = [(pe, None)]
                if len(convd) >= 2:
                    hw.append((dv, convd[-2]))
                em.emit("vector", lambda e, ps=ps, c4=c4, gb=gb, j0=j0, j1=j1,
                        t0=t0: e.scalar_tensor_tensor(
                    hgr[:, gb + c4 * 130 + j0:gb + c4 * 130 + j1],
                    ps[:, j0:j1], sb["iwcol"][:, 0:1],
                    ivfm[:, t0 + j0:t0 + j1], ALU.mult, ALU.mult),
                    waits=hw)
                if i == 0:
                    em.emit("vector", lambda e, c4=c4, gb=gb: e.memset(
                        hgr[:, gb + c4 * 130:gb + c4 * 130 + 1], 0.0))
                if i == NT - 1:
                    em.emit("vector", lambda e, c4=c4, gb=gb: e.memset(
                        hgr[:, gb + c4 * 130 + 129:gb + c4 * 130 + 130], 0.0))
                hevac.append(dv.n)
                if c4 == 3:
                    def hv(j, gb=gb):
                        return bass.AP(hgr, gb + j, [[1040, 128], [130, 4],
                                                     [1, 128]])
                    def wmap(j, g=g):
                        return bass.AP(sb["cw"], 3 * (4 * g) + j,
                                       [[3 * HC, 128], [3, 4], [0, 128]])
                    cbm = bass.AP(sb["cb"], 4 * g,
                                  [[HC, 128], [1, 4], [0, 128]])
                    out3 = bass.AP(tcv, g * 512, [[HC * 128, 128], [128, 4],
                                                  [1, 128]])
                    tm3 = bass.AP(tmpg, 0, [[512, 128], [128, 4], [1, 128]])
                    em.emit("vector", lambda e, hv=hv, wmap=wmap, out3=out3:
                            e.tensor_tensor(out3, hv(1), wmap(1), ALU.mult))
                    em.emit("vector", lambda e, cbm=cbm, out3=out3:
                            e.tensor_tensor(out3, out3, cbm, ALU.add))
                    em.emit("vector", lambda e, hv=hv, wmap=wmap, tm3=tm3:
                            e.tensor_tensor(tm3, hv(0), wmap(0), ALU.mult))
                    em.emit("vector", lambda e, out3=out3, tm3=tm3:
                            e.tensor_tensor(out3, out3, tm3, ALU.add))
                    em.emit("vector", lambda e, hv=hv, wmap=wmap, tm3=tm3:
                            e.tensor_tensor(tm3, hv(2), wmap(2), ALU.mult))
                    em.emit("vector", lambda e, out3=out3, tm3=tm3:
                            e.tensor_tensor(out3, out3, tm3, ALU.add))
                    convd.append(dv.n)
            em.emit("scalar", lambda e: e.activation(tcv[:], tcv[:], AF.Silu),
                    waits=[(dv, None)])
            em.emit("vector", lambda e: e.tensor_reduce(
                gmax[:], rearr(tcv, "p (c t) -> p t c", c=HC), AX.X, ALU.max,
                apply_absolute_value=True), waits=[(ac, None)])
            em.emit("tensor", lambda e: e.transpose(
                tmps[:], gmax[:], sb["eye32"][:]), inc=pe,
                waits=[(dv, None), (dv, None)])
            em.emit("vector", lambda e: e.reduce_max(
                sb["sdc"][:, 2:3], tmps[:], AX.X), waits=[(pe, None)])
            em.emit("vector", lambda e: e.tensor_scalar_max(
                sb["sdc"][:, 3:4], sb["sdc"][:, 2:3], 1e-5))
            em.emit("vector", lambda e: e.reciprocal(
                sb["sdc"][:, 4:5], sb["sdc"][:, 3:4]))
            em.emit("vector", lambda e: e.tensor_scalar_mul(
                sb["sdc"][:, 5:6], sb["sdc"][:, 4:5], 127.0))
            em.emit("vector", lambda e: e.tensor_scalar_mul(
                sb["sdc"][:, 6:7], sb["sdc"][:, 3:4], 1.0 / 127.0))
            em.emit("tensor", lambda e: e.transpose(
                tmps[0:1, :], sb["sdc"][:, 5:6], sb["eye32"][:]), inc=pe,
                waits=[(dv, None)])
            em.emit("vector", lambda e: e.tensor_copy(
                sb["ivrow"][0:1, :], tmps[0:1, :]), waits=[(pe, None)])
            em.emit("tensor", lambda e: e.matmul(
                tmps[:], sb["ones32"][0:1, :], sb["ivrow"][0:1, :],
                start=True, stop=True), inc=pe, waits=[(dv, None)])
            em.emit("vector", lambda e: e.tensor_copy(sgb[:], tmps[:]),
                    waits=[(pe, None)])
            sgv = bass.AP(sgb, 0, [[128, 128], [0, 4], [1, 128]])
            silu_ac = ctx["ac"].n
            sgb_dv = dv.n
            del silu_ac
            silu_ac = ctx["ac"].n
            dg = 0
            for g in range(8):
                gq4 = gqb[:, (g % 2) * 512:(g % 2) * 512 + 512]
                ts = tcv[:, g * 512:(g + 1) * 512]
                yw = [(ac, silu_ac), (dv, sgb_dv), (ctx["gc"], dg)]
                if i >= 1 and g == 0:
                    yw.append((sf, fspill[i - 1]))
                em.emit("gpsimd", lambda e, ts=ts: e.tensor_tensor(
                    rearr_ap3(yb), rearr_ap3(ts), sgv, ALU.mult),
                    inc=ctx["gc"], waits=yw)
                yg = ctx["gc"].n
                em.emit("vector", lambda e: e.tensor_copy(qi2[:], yb[:]),
                        waits=[(ctx["gc"], yg)])
                em.emit("vector", lambda e: e.tensor_scalar(
                    r02[:], qi2[:], -1.0, None, ALU.add))
                r0d = dv.n
                em.emit("gpsimd", lambda e: e.tensor_sub(db2[:], yb[:], r02[:]),
                        inc=ctx["gc"], waits=[(dv, r0d)])
                dg = ctx["gc"].n
                em.emit("vector", lambda e: e.scalar_tensor_tensor(
                    f12[:], db2[:], 1.5, r02[:], ALU.is_gt, ALU.add),
                    waits=[(ctx["gc"], dg)])
                ow = []
                if (i, g - 2) in w2done:
                    ow.append((pe, w2done[(i, g - 2)]))
                em.emit("vector", lambda e, gq4=gq4: e.scalar_tensor_tensor(
                    gq4, db2[:], 0.5, f12[:], ALU.is_ge, ALU.add), waits=ow)
                qd = dv.n
                ps = fps[i % 2]
                for cc4 in range(4):
                    cc = g * 4 + cc4
                    w2w = [(dv, qd)]
                    if i >= 2 and cc == 0:
                        w2w.append((dv, fevac[i - 2]))
                    for nb in range(2):
                        em.emit("tensor", lambda e, ps=ps, cc=cc, nb=nb,
                                gq4=gq4, cc4=cc4: e.matmul(
                            ps[:, nb * 512:(nb + 1) * 512],
                            gq4[:, cc4 * 128:(cc4 + 1) * 128],
                            w2q[:, cc * D + nb * 512:cc * D + (nb + 1) * 512],
                            start=(cc == 0), stop=(cc == HC - 1)),
                            inc=pe if nb == 1 else None,
                            waits=w2w if (nb == 0 and cc4 == 0) else [])
                w2done[(i, g)] = pe.n
            fw = [(pe, None)]
            if i >= 1:
                fw.append((sf, fspill[i - 1]))
            em.emit("vector", lambda e, i=i: e.tensor_scalar(
                ft[:], fps[i % 2][:], sb["sdc"][:, 6:7], sb["iwcol"][:, 1:2],
                ALU.mult, ALU.mult), waits=fw)
            fevac.append(dv.n)
            em.emit("gpsimd", lambda e, i=i: e.dma_start(
                dr["fscr"][i * 128:(i + 1) * 128, :], ft[:]),
                inc=sf, k=16, waits=[(dv, None)])
            fspill.append(sf.n)
        em.play()

    with (
        nc.sbuf_tensor(f"s{s}_frl", [128, 2 * D], F32) as frl,
        nc.sbuf_tensor(f"s{s}_x2rl", [128, 2 * D], F32) as x2rl,
        nc.sbuf_tensor(f"s{s}_fn", [128, D], F32) as fn,
        nc.sbuf_tensor(f"s{s}_outt", [128, 2 * D], F32) as outt,
    ):
        srf = (ctx["srf0"], ctx["srf1"])
        srx = (ctx["srx0"], ctx["srx1"])
        so = (ctx["so0"], ctx["so1"])
        used = []
        outdma = []
        for i in range(NT):
            fb = frl[:, (i % 2) * D:(i % 2) * D + D]
            xb = x2rl[:, (i % 2) * D:(i % 2) * D + D]
            w = [(sf, fspill[NT - 1]),
                 (ctx["ss0"], None), (ctx["ss1"], None)]
            if i >= 2:
                w.append((dv, used[i - 2]))
            em.emit("sync", lambda e, i=i, fb=fb: e.dma_start(
                fb, dr["fscr"][i * 128:(i + 1) * 128, :]),
                inc=srf[i % 2], k=16, waits=w)
            em.emit("sync", lambda e, i=i, xb=xb: e.dma_start(
                xb, dr["x2scr"][s, i * 128:(i + 1) * 128, :]),
                inc=srx[i % 2], k=16)
            _ln_stats(ctx, fb, w=[(srf[i % 2], None)])
            _ln_rsig(ctx)
            em.emit("vector", lambda e, fb=fb: e.tensor_scalar(
                fn[:], fb, sb["st2"][:, 0:1], sb["sdc"][:, 1:2],
                ALU.subtract, ALU.mult))
            ow = [(srx[i % 2], None)]
            if i >= 2:
                ow.append((so[i % 2], outdma[i - 2]))
            em.emit("vector", lambda e, xb=xb, i=i: e.tensor_add(
                outt[:, (i % 2) * D:(i % 2) * D + D], fn[:], xb), waits=ow)
            used.append(dv.n)
            em.emit("gpsimd", lambda e, i=i: e.dma_start(
                dr["out"][s, i * 128:(i + 1) * 128, :],
                outt[:, (i % 2) * D:(i % 2) * D + D]),
                inc=so[i % 2], k=16, waits=[(dv, None)])
            outdma.append(so[i % 2].n)
        em.play()


def _host_inputs(inputs):
    f32 = np.float32
    wq = np.ascontiguousarray(
        np.asarray(inputs["L0_attn_wq"], f32).transpose(1, 0, 2).reshape(D, D))
    wk = np.ascontiguousarray(
        np.asarray(inputs["L0_attn_wk"], f32).transpose(1, 0, 2).reshape(D, D))
    wv = np.ascontiguousarray(
        np.asarray(inputs["L0_attn_wv"], f32).transpose(1, 0, 2).reshape(D, D))
    wo = np.ascontiguousarray(np.asarray(inputs["L0_attn_wo"], f32).T)
    w1 = np.ascontiguousarray(np.asarray(inputs["L0_ffn_w1"], f32).T)
    w2 = np.ascontiguousarray(np.asarray(inputs["L0_ffn_w2"], f32).T)
    bq = np.ascontiguousarray(
        np.asarray(inputs["L0_attn_bq"], f32).reshape(D).reshape(DC, 128).T)
    bk = np.ascontiguousarray(
        np.asarray(inputs["L0_attn_bk"], f32).reshape(D).reshape(DC, 128).T)
    bv = np.ascontiguousarray(np.broadcast_to(
        np.asarray(inputs["L0_attn_bv"], f32).reshape(1, D), (128, D)))
    bo = np.ascontiguousarray(np.broadcast_to(
        np.asarray(inputs["L0_attn_bo"], f32).reshape(1, D), (128, D)))
    cw = np.asarray(inputs["L0_ffn_conv_w"], f32).reshape(DH, 3)
    cw_l = np.ascontiguousarray(
        cw.reshape(HC, 128, 3).transpose(1, 0, 2).reshape(128, 3 * HC))
    cb_l = np.ascontiguousarray(
        np.asarray(inputs["L0_ffn_conv_b"], f32).reshape(HC, 128).T)
    eye16 = np.eye(128, dtype=ml_dtypes.bfloat16)
    eye32 = np.eye(128, dtype=f32)
    return {
        "wq_t": wq, "wk_t": wk, "wv_t": wv, "wo_t": wo, "w1_t": w1, "w2_t": w2,
        "bq_l": bq, "bk_l": bk, "bv_b": bv, "bo_b": bo,
        "cw_l": cw_l, "cb_l": cb_l, "eye16": eye16, "eye32": eye32,
    }


def kernel(**inputs):
    if "nc" not in _CACHE:
        _CACHE["nc"] = build_nc()
    nc = _CACHE["nc"]
    shared = _host_inputs(inputs)
    x = np.asarray(inputs["L0_attn_x"], np.float32)
    in_maps = []
    for c in range(NCORES):
        m = dict(shared)
        m["x"] = np.ascontiguousarray(x[c * SEQS:(c + 1) * SEQS])
        in_maps.append(m)
    res = run_bass_kernel_spmd(nc, in_maps, core_ids=list(range(NCORES)))
    return np.concatenate([r["out"] for r in res.results], axis=0)


# version 16 (final)
# speedup vs baseline: 1.0624x; 1.0260x over previous
import numpy as np
import ml_dtypes
import concourse.bass as bass
import concourse.mybir as mybir
from concourse.bass_utils import run_bass_kernel_spmd

F32 = mybir.dt.float32
BF16 = mybir.dt.bfloat16
I32 = mybir.dt.int32
AF = mybir.ActivationFunctionType
ALU = mybir.AluOpType
AX = mybir.AxisListType

NCORES = 8
SEQS = 2
S = 2048
D = 1024
H = 16
E = 64
DH = 4096
NT = S // 128
DC = D // 128
HC = DH // 128
VW = H * (E + 1)

ENGS = ("sync", "gpsimd", "tensor", "vector", "scalar")
USE_LNEXP_IV = False

_CACHE = {}


class Sem:
    def __init__(self, nc, name):
        self.h = nc.alloc_semaphore(name)
        self.n = 0


class Em:

    def __init__(self, nc, sems):
        self.nc = nc
        self.sems = sems
        self.reset()

    def reset(self):
        self.streams = {e: [] for e in ENGS}
        self.marks = {e: {} for e in ENGS}

    def emit(self, eng, fn, inc=None, k=1, waits=()):
        if eng == "vector":
            waits = list(waits) + [(self.sems["dv"], None)]
            if inc is None:
                inc, k = self.sems["dv"], 1
        elif eng == "scalar":
            waits = list(waits) + [(self.sems["ac"], None)]
            if inc is None:
                inc, k = self.sems["ac"], 1
        wl = []
        m = self.marks[eng]
        for sem, val in waits:
            if val is None:
                val = sem.n
            if val <= 0:
                continue
            if m.get(id(sem), -1) >= val:
                continue
            m[id(sem)] = val
            wl.append((sem.h, val))
        if inc is not None:
            inc.n += k

        def play(e, fn=fn, wl=wl, inc=inc, k=k):
            for h, v in wl:
                e.wait_ge(h, v)
            ins = fn(e)
            if inc is not None and ins is not None:
                ins.then_inc(inc.h, k)
        self.streams[eng].append(play)
        return inc.n if inc is not None else None

    def play(self):
        with self.nc.Block() as blk:
            for name in ENGS:
                fns = self.streams[name]
                if not fns:
                    continue

                def body(e, fns=fns):
                    for f in fns:
                        f(e)
                getattr(blk, name)(body)
        self.reset()


def rearr(t, spec, **kw):
    return t[:].rearrange(spec, **kw)


def rearr_ap3(t_or_ap):
    return rearr(t_or_ap, "p (c t) -> p c t", c=4)


def build_nc():
    nc = bass.Bass("TRN2", target_bir_lowering=False)

    dr = {}
    dr["x"] = nc.dram_tensor("x", [SEQS, S, D], F32, kind="ExternalInput")
    for n, shp in [("wq_t", [D, D]), ("wk_t", [D, D]), ("wv_t", [D, D]),
                   ("wo_t", [D, D]), ("w1_t", [D, DH]), ("w2_t", [DH, D]),
                   ("bq_l", [128, DC]), ("bk_l", [128, DC]),
                   ("bv_b", [128, D]), ("bo_b", [128, D]),
                   ("cw_l", [128, 3 * HC]), ("cb_l", [128, HC]),
                   ("eye32", [128, 128])]:
        dr[n] = nc.dram_tensor(n, shp, F32, kind="ExternalInput")
    dr["eye16"] = nc.dram_tensor("eye16", [128, 128], BF16, kind="ExternalInput")
    dr["out"] = nc.dram_tensor("out", [SEQS, S, D], F32, kind="ExternalOutput")
    dr["x2scr"] = nc.dram_tensor("x2scr", [SEQS, S, D], F32, kind="Internal")
    dr["fscr"] = nc.dram_tensor("fscr", [S, D], F32, kind="Internal")
    dr["w1qscr"] = nc.dram_tensor("w1qscr", [128, DC * DH], BF16, kind="Internal")
    dr["w2qscr"] = nc.dram_tensor("w2qscr", [128, HC * D], BF16, kind="Internal")

    sb = {}
    sb["ones32"] = nc.alloc_sbuf_tensor("ones32", [128, 128], F32)
    sb["ones16"] = nc.alloc_sbuf_tensor("ones16", [128, 128], BF16)
    sb["eye16"] = nc.alloc_sbuf_tensor("eye16s", [128, 128], BF16)
    sb["eye32"] = nc.alloc_sbuf_tensor("eye32s", [128, 128], F32)
    sb["bq"] = nc.alloc_sbuf_tensor("bq_sb", [128, DC], F32)
    sb["bk"] = nc.alloc_sbuf_tensor("bk_sb", [128, DC], F32)
    sb["bv"] = nc.alloc_sbuf_tensor("bv_bc", [128, D], BF16)
    sb["bo"] = nc.alloc_sbuf_tensor("bo_bc", [128, D], BF16)
    sb["cw"] = nc.alloc_sbuf_tensor("cw_sb", [128, 3 * HC], F32)
    sb["cb"] = nc.alloc_sbuf_tensor("cb_sb", [128, HC], F32)
    sb["swcol"] = nc.alloc_sbuf_tensor("swcol", [128, 2], F32)
    sb["iwcol"] = nc.alloc_sbuf_tensor("iwcol", [128, 2], F32)
    sb["invsx16"] = nc.alloc_sbuf_tensor("invsx16", [128, NT], F32)
    sb["st6"] = nc.alloc_sbuf_tensor("st6", [128, 12], F32)
    sb["st2"] = nc.alloc_sbuf_tensor("st2", [128, 2], F32)
    sb["sdc"] = nc.alloc_sbuf_tensor("sdc", [128, 8], F32)
    sb["acc2"] = nc.alloc_sbuf_tensor("acc2", [128, 2], F32)
    sb["t1c"] = nc.alloc_sbuf_tensor("t1c", [128, 2], F32)
    sb["srow"] = nc.alloc_sbuf_tensor("srow", [1, 4], F32)
    sb["epsc"] = nc.alloc_sbuf_tensor("epsc", [128, 1], F32)
    nc.const_aps.register(F32, 1e-5, sb["epsc"][:])
    sb["ivrow"] = nc.alloc_sbuf_tensor("ivrow", [1, 128], F32)

    sems = {k: Sem(nc, k) for k in
            ("scst", "gcst", "sw0", "sw1", "sx0", "sx1", "gw0", "gw1",
             "sqa", "ska", "sod", "ss0", "ss1", "sws", "sf",
             "srf0", "srf1", "srx0", "srx1", "so0", "so1",
             "pe", "dv", "ac", "gc")}
    em = Em(nc, sems)
    ctx = {"nc": nc, "dr": dr, "sb": sb, "em": em, **sems}

    ctx["use_lnexp_iv"] = USE_LNEXP_IV
    _p0(ctx)
    for s in range(SEQS):
        _seq(ctx, s)

    with nc.Block() as blk:
        @blk.sync
        def _(sync):
            sync.wait_ge(sems["so0"].h, sems["so0"].n)
            sync.wait_ge(sems["so1"].h, sems["so1"].n)
    return nc


def _p0(ctx):
    nc, dr, sb, em = ctx["nc"], ctx["dr"], ctx["sb"], ctx["em"]
    pe, dv, gc = ctx["pe"], ctx["dv"], ctx["gc"]
    scst, gcst = ctx["scst"], ctx["gcst"]
    sw = (ctx["sw0"], ctx["sw1"])

    for name, dst in [("eye16", "eye16"), ("eye32", "eye32"),
                      ("bq_l", "bq"), ("bk_l", "bk"),
                      ("cw_l", "cw"), ("cb_l", "cb")]:
        em.emit("sync", lambda e, a=dr[name], b=sb[dst]: e.dma_start(b[:], a[:]),
                inc=scst, k=16)
    for name, dst in [("bv_b", "bv"), ("bo_b", "bo")]:
        em.emit("gpsimd", lambda e, a=dr[name], b=sb[dst]: e.dma_start(b[:], a[:]),
                inc=gcst, k=16)
    em.emit("gpsimd", lambda e: e.memset(sb["ones32"][:], 1.0))
    em.emit("gpsimd", lambda e: e.memset(sb["ones16"][:], 1.0))
    em.emit("gpsimd", lambda e: e.memset(sb["epsc"][:], 1e-5))
    em.emit("gpsimd", lambda e: e.memset(sb["acc2"][:], 0.0), inc=gc)
    em.play()

    with (
        nc.sbuf_tensor("p0st", [128, 2 * DH], F32) as p0st,
        nc.psum_tensor("p0ps", [128, 8], F32) as p0ps,
    ):
        chunks = [("w1_t", c * 128, DH, 0) for c in range(DC)] + \
                 [("w2_t", c * 128, D, 1) for c in range(HC)]
        red_done = []
        for j, (src, r0, ncol, col) in enumerate(chunks):
            w = [(dv, red_done[j - 2])] if j >= 2 else []
            em.emit("sync",
                    lambda e, src=dr[src], r0=r0, ncol=ncol, j=j:
                    e.dma_start(p0st[:, (j % 2) * DH:(j % 2) * DH + ncol],
                                src[r0:r0 + 128, :]),
                    inc=sw[j % 2], k=16, waits=w)
            em.emit("vector",
                    lambda e, ncol=ncol, j=j, col=col:
                    e.reduce_sum(sb["t1c"][:, col:col + 1],
                                 p0st[:, (j % 2) * DH:(j % 2) * DH + ncol],
                                 AX.X, apply_absolute_value=True),
                    waits=[(sw[j % 2], None)])
            em.emit("vector",
                    lambda e, col=col:
                    e.tensor_add(sb["acc2"][:, col:col + 1],
                                 sb["acc2"][:, col:col + 1],
                                 sb["t1c"][:, col:col + 1]),
                    waits=[(gc, None)])
            red_done.append(dv.n)
        em.emit("tensor", lambda e: e.matmul(p0ps[0:1, 0:2], sb["ones32"][:, 0:1],
                                             sb["acc2"][:], start=True, stop=True),
                inc=pe, waits=[(dv, None)])
        em.emit("vector", lambda e: e.tensor_scalar(
            sb["srow"][0:1, 0:2], p0ps[0:1, 0:2], 1.0 / (D * DH), 1e-5,
            ALU.mult, ALU.max), waits=[(pe, None)])
        em.emit("vector", lambda e: e.reciprocal(
            sb["srow"][0:1, 2:4], sb["srow"][0:1, 0:2]))
        em.emit("tensor", lambda e: e.matmul(p0ps[:, 2:4], sb["ones32"][0:1, :],
                                             sb["srow"][0:1, 2:4],
                                             start=True, stop=True),
                inc=pe, waits=[(dv, None)])
        em.emit("tensor", lambda e: e.matmul(p0ps[:, 4:6], sb["ones32"][0:1, :],
                                             sb["srow"][0:1, 0:2],
                                             start=True, stop=True), inc=pe)
        em.emit("vector", lambda e: e.tensor_copy(sb["swcol"][:], p0ps[:, 2:4]),
                waits=[(pe, None)])
        em.emit("vector", lambda e: e.tensor_copy(sb["iwcol"][:], p0ps[:, 4:6]))
        em.play()


def _ln_stats(ctx, src_ap, w=()):
    em, sb = ctx["em"], ctx["sb"]
    em.emit("vector", lambda e: e.bn_stats(sb["st6"][:, 0:6], src_ap[:, 0:512]),
            waits=list(w))
    em.emit("vector", lambda e: e.bn_stats(sb["st6"][:, 6:12],
                                           src_ap[:, 512:1024]))
    em.emit("vector", lambda e: e.bn_aggr(sb["st2"][:], sb["st6"][:]))


def _ln_rsig(ctx):
    em, dv, sb = ctx["em"], ctx["dv"], ctx["sb"]
    em.emit("scalar", lambda e: e.activation(sb["sdc"][:, 0:1], sb["st2"][:, 1:2],
                                             AF.Sqrt, bias=1e-5),
            waits=[(dv, None)])
    em.emit("vector", lambda e: e.reciprocal(sb["sdc"][:, 1:2], sb["sdc"][:, 0:1]),
            waits=[(ctx["ac"], None)])


def _round_to_bf16(ctx, y_ap, out_ap, scr, outw=()):
    em = ctx["em"]
    em.emit("vector", lambda e: e.tensor_copy(scr["i32"], y_ap))
    em.emit("vector", lambda e: e.tensor_scalar(scr["r0"], scr["i32"], -1.0,
                                                None, ALU.add))
    em.emit("vector", lambda e: e.tensor_sub(scr["d"], y_ap, scr["r0"]))
    em.emit("vector", lambda e: e.scalar_tensor_tensor(scr["f1"], scr["d"], 1.5,
                                                       scr["r0"], ALU.is_gt,
                                                       ALU.add))
    em.emit("vector", lambda e: e.scalar_tensor_tensor(out_ap, scr["d"], 0.5,
                                                       scr["f1"], ALU.is_ge,
                                                       ALU.add),
            waits=list(outw))


def _seq(ctx, s):
    nc = ctx["nc"]
    with (
        nc.sbuf_tensor(f"s{s}_xqfm", [128, DC * S], BF16) as xqfm,
        nc.sbuf_tensor(f"s{s}_ivfm", [128, S], F32) as ivfm,
    ):
        with nc.sbuf_tensor(f"s{s}_big32", [128, DC * S], BF16) as big32:
            with (
                nc.sbuf_tensor(f"s{s}_qpk", [128, DC * S], BF16) as qpk,
                nc.sbuf_tensor(f"s{s}_kpk", [128, DC * S], BF16) as kpk,
                nc.sbuf_tensor(f"s{s}_vsb", [128, NT * VW], BF16) as vsb,
            ):
                _p1_ln1(ctx, s, big32)
                _p2_qkv(ctx, s, big32, qpk, kpk, vsb, (xqfm, [0, DC * D]))
                with (
                    nc.sbuf_tensor(f"s{s}_pt", [128, 3 * S], BF16) as pt,
                    nc.sbuf_tensor(f"s{s}_oddh", [64, S], BF16) as oddh,
                    nc.sbuf_tensor(f"s{s}_qa", [65, 2 * S], BF16) as qa,
                    nc.sbuf_tensor(f"s{s}_ka", [65, 2 * S], BF16) as ka,
                    nc.sbuf_tensor(f"s{s}_mke", [64, 65], BF16) as mke,
                ):
                    ctx["ivb16"] = xqfm
                    ctx["absq"] = oddh
                    _p3_attn(ctx, s, qpk, kpk, vsb, big32, pt, ivfm, oddh,
                             qa, ka, mke)
            _p4_proj_ln(ctx, s, big32, xqfm, ivfm)
        with (
            nc.sbuf_tensor(f"s{s}_w1q", [128, DC * DH], BF16) as w1q,
            nc.sbuf_tensor(f"s{s}_w2q", [128, HC * D], BF16) as w2q,
        ):
            _p45_wquant(ctx, s, w1q, w2q)
            _p5_ffn(ctx, s, xqfm, ivfm, w1q, w2q)


def _p1_ln1(ctx, s, xn1):
    nc, dr, sb, em = ctx["nc"], ctx["dr"], ctx["sb"], ctx["em"]
    pe, dv = ctx["pe"], ctx["dv"]
    sx = (ctx["sx0"], ctx["sx1"])
    with (
        nc.sbuf_tensor(f"s{s}_xint", [128, 2 * D], F32) as xint,
        nc.sbuf_tensor(f"s{s}_xn1t", [128, D], BF16) as xn1t,
        nc.psum_tensor(f"s{s}_p1ps", [128, D], BF16) as p1ps,
    ):
        consumed = []
        for i in range(NT):
            xb = xint[:, (i % 2) * D:(i % 2) * D + D]
            w = [(dv, consumed[i - 2])] if i >= 2 else []
            if i == 0:
                w += [(ctx["so0"], None), (ctx["so1"], None),
                      (ctx["sws"], None), (ctx["sf"], None)]
            em.emit("sync", lambda e, i=i, xb=xb: e.dma_start(
                xb, dr["x"][s, i * 128:(i + 1) * 128, :]),
                inc=sx[i % 2], k=16, waits=w)
            _ln_stats(ctx, xb, w=[(sx[i % 2], None), (ctx["scst"], None),
                                  (ctx["gcst"], None)])
            _ln_rsig(ctx)
            em.emit("vector", lambda e, xb=xb: e.tensor_scalar(
                xn1t[:], xb, sb["st2"][:, 0:1], sb["sdc"][:, 1:2],
                ALU.subtract, ALU.mult), waits=[(pe, None)])
            consumed.append(ctx["dv"].n)
            for c in range(DC):
                em.emit("tensor", lambda e, c=c: e.transpose(
                    p1ps[:, c * 128:(c + 1) * 128],
                    xn1t[:, c * 128:(c + 1) * 128], sb["eye16"][:]),
                    inc=pe if c == DC - 1 else None,
                    waits=[(dv, None)] if c == 0 else [])
            em.emit("vector", lambda e, i=i: e.tensor_copy(
                rearr(xn1, "p (c t) -> p c t", c=DC)[:, :, i * 128:(i + 1) * 128],
                rearr(p1ps, "p (c t) -> p c t", c=DC)),
                waits=[(pe, None)])
        em.play()


def _p2_qkv(ctx, s, xn1, qpk, kpk, vsb, wbufs):
    nc, dr, sb, em = ctx["nc"], ctx["dr"], ctx["sb"], ctx["em"]
    pe, dv, gc = ctx["pe"], ctx["dv"], ctx["gc"]
    gw = (ctx["gw0"], ctx["gw1"])
    wtens, wofs = wbufs

    def wload(idx, name, waits=()):
        src = rearr(dr[name], "(dc p) he -> p dc he", p=128)
        em.emit("gpsimd", lambda e, src=src, idx=idx: e.dma_start(
            rearr(wtens, "p (g dc he) -> p g dc he", g=2, dc=DC)[:, idx],
            src),
            inc=gw[idx], k=16, waits=list(waits))
        return gw[idx].n

    with (
        nc.psum_tensor(f"s{s}_qps0", [128, S], F32) as qps0,
        nc.psum_tensor(f"s{s}_qps1", [128, S], F32) as qps1,
    ):
        qps = (qps0, qps1)
        gwq = wload(0, "wq_t", waits=[(ctx["so0"], None), (ctx["so1"], None),
                                       (ctx["sws"], None), (ctx["sf"], None)])
        gwk = wload(1, "wk_t")
        gwv = [None]
        evac_done = {}
        for proj, (dst, bias, scal, wb, gwn) in enumerate(
                [(qpk, "bq", 0.125, 0, gwq), (kpk, "bk", None, 1, gwk)]):
            for oc in range(DC):
                ps = qps[oc % 2]
                w = [(gw[wb], gwn)]
                prev = (proj, oc - 2)
                if oc < 2:
                    prev = (proj - 1, oc + DC - 2)
                if prev in evac_done:
                    w.append((dv, evac_done[prev]))
                for dc in range(DC):
                    for nb in range(4):
                        em.emit("tensor", lambda e, ps=ps, oc=oc, dc=dc, nb=nb,
                                wo=wofs[wb]: e.matmul(
                            ps[:, nb * 512:(nb + 1) * 512],
                            wtens[:, wo + dc * D + oc * 128:wo + dc * D + oc * 128 + 128],
                            xn1[:, dc * S + nb * 512:dc * S + (nb + 1) * 512],
                            start=(dc == 0), stop=(dc == DC - 1)),
                            inc=pe if (dc == DC - 1 and nb == 3) else None,
                            waits=w if (dc == 0 and nb == 0) else [])
                if scal is None:
                    em.emit("vector", lambda e, ps=ps, oc=oc, dst=dst, bias=bias:
                            e.tensor_scalar(dst[:, oc * S:(oc + 1) * S], ps[:],
                                            sb[bias][:, oc:oc + 1], None,
                                            ALU.add),
                            waits=[(pe, None)])
                else:
                    em.emit("vector", lambda e, ps=ps, oc=oc, dst=dst, bias=bias,
                            sc=scal: e.tensor_scalar(
                        dst[:, oc * S:(oc + 1) * S], ps[:],
                        sb[bias][:, oc:oc + 1], sc, ALU.add, ALU.mult),
                        waits=[(pe, None)])
                evac_done[(proj, oc)] = dv.n
            if proj == 0:
                gwv[0] = wload(0, "wv_t", waits=[(pe, None)])
        em.play()

    for kt in range(NT):
        em.emit("gpsimd", lambda e, kt=kt: e.memset(
            rearr(vsb, "p (kt h e) -> p kt h e", kt=NT, h=H)[:, kt, :, 64:65],
            1.0), inc=gc if kt == NT - 1 else None)
    with (
        nc.psum_tensor(f"s{s}_vps0", [128, D], F32) as vps0,
        nc.psum_tensor(f"s{s}_vps1", [128, D], F32) as vps1,
    ):
        vps = (vps0, vps1)
        evd = []
        for tt in range(NT):
            ps = vps[tt % 2]
            w = [(ctx["gw0"], gwv[0])]
            if tt >= 2:
                w.append((dv, evd[tt - 2]))
            for dc in range(DC):
                for nb in range(2):
                    em.emit("tensor", lambda e, ps=ps, tt=tt, dc=dc, nb=nb:
                            e.matmul(
                        ps[:, nb * 512:(nb + 1) * 512],
                        xn1[:, dc * S + tt * 128:dc * S + tt * 128 + 128],
                        wtens[:, wofs[0] + dc * D + nb * 512:wofs[0] + dc * D + (nb + 1) * 512],
                        start=(dc == 0), stop=(dc == DC - 1)),
                        inc=pe if (dc == DC - 1 and nb == 1) else None,
                        waits=w if (dc == 0 and nb == 0) else [])
            em.emit("vector", lambda e, ps=ps, tt=tt: e.tensor_tensor(
                rearr(vsb, "p (kt h e) -> p kt h e", kt=NT, h=H)[:, tt, :, 0:64],
                rearr(ps, "p (h e) -> p h e", h=H),
                rearr(sb["bv"], "p (h e) -> p h e", h=H), ALU.add),
                waits=[(pe, None), (gc, None)])
            evd.append(dv.n)
        em.play()


def _p3_attn(ctx, s, qpk, kpk, vsb, attn, pt, ivr, oddh, qa, ka, mke):
    nc, sb, em = ctx["nc"], ctx["sb"], ctx["em"]
    pe, dv, ac, gc = ctx["pe"], ctx["dv"], ctx["ac"], ctx["gc"]
    sqa, ska, sod = ctx["sqa"], ctx["ska"], ctx["sod"]
    absq = ctx["absq"]
    Q = 1024

    with (
        nc.psum_tensor(f"s{s}_sc0", [128, Q], F32) as sc0,
        nc.psum_tensor(f"s{s}_sc1", [128, Q], F32) as sc1,
        nc.psum_tensor(f"s{s}_atps", [65, S], F32) as atps,
    ):
        scp = (sc0, sc1)
        em.emit("gpsimd", lambda e: e.memset(ka[64:65, :], 1.0))
        em.emit("gpsimd", lambda e: e.memset(mke[:], 0.0), inc=gc)

        odd_dma = {}
        attv_last = {}
        attv_cnt = {}
        exp_cnt = {}
        mult_done = {}
        sc_cnt = {}

        def aug_prep(h):
            if h >= H:
                return
            c, half, b = h // 2, h % 2, h % 2
            qsl = slice(c * S, (c + 1) * S)
            wbuf = []
            if h >= 2:
                wbuf.append((pe, attv_last[h - 2]))
            for src, dstt, dsem in ((qpk, qa, sqa), (kpk, ka, ska)):
                if half == 0:
                    em.emit("vector", lambda e, src=src, dstt=dstt, qsl=qsl, b=b:
                            e.tensor_copy(dstt[0:64, b * S:(b + 1) * S],
                                          src[0:64, qsl]),
                            waits=wbuf)
                else:
                    em.emit("sync", lambda e, src=src, dstt=dstt, qsl=qsl, b=b:
                            e.dma_start(dstt[0:64, b * S:(b + 1) * S],
                                        src[64:128, qsl]),
                            inc=dsem, k=16, waits=wbuf)
            em.emit("vector", lambda e, b=b: e.reduce_max(
                mke[:, 64:65], ka[0:64, b * S:(b + 1) * S], AX.X,
                apply_absolute_value=True),
                waits=[(ska, None), (gc, None), (pe, None)])

        def emit_absq(h):
            if h >= H:
                return
            b = h % 2
            em.emit("vector", lambda e, b=b: e.tensor_scalar_mul(
                absq[0:64, :], qa[0:64, b * S:(b + 1) * S], -1.0),
                waits=[(sqa, None), (sod, None)])
            em.emit("vector", lambda e, b=b: e.tensor_tensor(
                absq[0:64, :], absq[0:64, :], qa[0:64, b * S:(b + 1) * S],
                ALU.max))

        def emit_M(h, j):
            for nb in range(2):
                em.emit("tensor", lambda e, nb=nb, j=j: e.matmul(
                    scp[j][0:65, nb * 512:(nb + 1) * 512], mke[:],
                    absq[0:64, j * Q + nb * 512:j * Q + (nb + 1) * 512],
                    start=True, stop=True),
                    inc=pe if nb == 1 else None,
                    waits=([(dv, None), (ac, exp_cnt.get((h - 1, NT - 1, j), 0))]
                           if nb == 0 else []))
            b = h % 2
            em.emit("vector", lambda e, b=b, j=j: e.tensor_scalar(
                qa[64:65, b * S + j * Q:b * S + (j + 1) * Q],
                scp[j][64:65, :], 80.0, -1.0,
                ALU.min, ALU.mult), waits=[(pe, None)])
            return dv.n

        def emit_scores(h, kt, j, negm):
            b = h % 2
            for nb in range(2):
                w = []
                if nb == 0:
                    if kt == 0:
                        w = [(dv, negm[j])]
                    else:
                        w = [(ac, exp_cnt[(h, kt - 1, j)])]
                    if kt == 7:
                        w.append((dv, mult_done.get(h - 1, 0)))
                em.emit("tensor", lambda e, kt=kt, nb=nb, b=b, j=j: e.matmul(
                    scp[j][:, nb * 512:(nb + 1) * 512],
                    ka[:, b * S + kt * 128:b * S + kt * 128 + 128],
                    qa[:, b * S + j * Q + nb * 512:b * S + j * Q + (nb + 1) * 512],
                    start=True, stop=True),
                    inc=pe if nb == 1 else None, waits=w)
            sc_cnt[(h, kt, j)] = pe.n

        def emit_exp(h, kt, j):
            p = kt % 3
            w = [(pe, sc_cnt[(h, kt, j)])]
            if kt >= 3:
                w.append((pe, attv_cnt.get((h, kt - 3, j), 0)))
            em.emit("scalar", lambda e, p=p, j=j: e.activation(
                pt[:, p * S + j * Q:p * S + (j + 1) * Q], scp[j][:], AF.Exp),
                waits=w)
            exp_cnt[(h, kt, j)] = ac.n

        def emit_attv(h, kt, j):
            p = kt % 3
            for nb in range(2):
                w = []
                if nb == 0:
                    w = [(ac, exp_cnt[(h, kt, j)])]
                    if kt == 0:
                        w.append((dv, ctx.get(f"_cpA{h - 1}", 0)))
                em.emit("tensor", lambda e, kt=kt, nb=nb, h=h, p=p, j=j: e.matmul(
                    atps[0:65, j * Q + nb * 512:j * Q + (nb + 1) * 512],
                    vsb[:, kt * VW + h * 65:kt * VW + (h + 1) * 65],
                    pt[:, p * S + j * Q + nb * 512:p * S + j * Q + (nb + 1) * 512],
                    start=(kt == 0), stop=(kt == NT - 1)),
                    inc=pe if nb == 1 else None, waits=w)
            attv_cnt[(h, kt, j)] = pe.n

        def emit_evac(h):
            em.emit("vector", lambda e: e.tensor_copy(ivr[0:65, :],
                                                      atps[0:65, :]),
                    waits=[(pe, attv_cnt[(h, NT - 1, 1)])])
            ctx[f"_cpA{h}"] = dv.n

        def emit_recip(h):
            if h < 0:
                return
            em.emit("vector", lambda e: e.reciprocal(
                ivr[64:65, :], ivr[64:65, :]))
            em.emit("vector", lambda e: e.tensor_copy(
                ctx["ivb16"][64:65, 0:S], ivr[64:65, :]))
            ctx[f"_iv{h}"] = dv.n

        def emit_post(h):
            if h < 0:
                return
            c, half = h // 2, h % 2
            for j in range(2):
                for nb in range(2):
                    w = []
                    if nb == 0:
                        w = [(dv, ctx[f"_iv{h}"]), (ac, None)]
                    em.emit("tensor", lambda e, nb=nb, j=j: e.matmul(
                        scp[j][0:64, nb * 512:(nb + 1) * 512],
                        sb["ones32"][64:65, 0:64],
                        ivr[64:65, j * Q + nb * 512:j * Q + (nb + 1) * 512],
                        start=True, stop=True),
                        inc=pe if nb == 1 else None, waits=w)
                dst = attn[0:64, c * S + j * Q:c * S + (j + 1) * Q] \
                    if half == 0 else oddh[0:64, j * Q:(j + 1) * Q]
                ww = [(pe, None)]
                if half == 1 and j == 0:
                    ww.append((sod, odd_dma.get(h - 2, 0)))
                em.emit("vector", lambda e, dst=dst, j=j: e.tensor_tensor(
                    dst, ivr[0:64, j * Q:(j + 1) * Q], scp[j][0:64, :],
                    ALU.mult), waits=ww)
            if half == 1:
                em.emit("sync", lambda e, c=c: e.dma_start(
                    attn[64:128, c * S:(c + 1) * S], oddh[0:64, :]),
                    inc=sod, k=16, waits=[(dv, None)])
                odd_dma[h] = sod.n
            mult_done[h] = dv.n

        aug_prep(0)
        emit_absq(0)
        for h in range(H):
            negm = [emit_M(h, 0), emit_M(h, 1)]
            aug_prep(h + 1)
            emit_recip(h - 1)
            for kt in range(3):
                for j in range(2):
                    emit_scores(h, kt, j, negm)
                    emit_exp(h, kt, j)
            emit_attv(h, 0, 0)
            emit_attv(h, 0, 1)
            for kt in range(3, NT):
                if kt == 7:
                    emit_post(h - 1)
                    emit_absq(h + 1)
                for j in range(2):
                    emit_scores(h, kt, j, negm)
                    emit_exp(h, kt, j)
                    emit_attv(h, kt - 2, j)
            for j in range(2):
                emit_attv(h, NT - 2, j)
                emit_attv(h, NT - 1, j)
            attv_last[h] = attv_cnt[(h, NT - 1, 1)]
            emit_evac(h)
        emit_recip(H - 1)
        emit_post(H - 1)
        em.play()
    ctx["attn_sod"] = ctx["sod"].n
def _p4_proj_ln(ctx, s, attn, xqfm, ivfm):
    nc, dr, sb, em = ctx["nc"], ctx["dr"], ctx["sb"], ctx["em"]
    pe, dv, ac = ctx["pe"], ctx["dv"], ctx["ac"]
    sx = (ctx["sx0"], ctx["sx1"])
    ss = (ctx["ss0"], ctx["ss1"])
    gw1 = ctx["gw1"]
    with (
        nc.sbuf_tensor(f"s{s}_wo", [128, DC * D], BF16) as wo,
        nc.sbuf_tensor(f"s{s}_xint2", [128, 2 * D], F32) as xint2,
        nc.sbuf_tensor(f"s{s}_ao", [128, D], F32) as ao,
        nc.sbuf_tensor(f"s{s}_x2t", [128, 2 * D], F32) as x2t,
        nc.sbuf_tensor(f"s{s}_xn2", [128, D], F32) as xn2,
        nc.sbuf_tensor(f"s{s}_yq", [128, D], F32) as yq,
        nc.sbuf_tensor(f"s{s}_qi32", [128, D], I32) as qi32,
        nc.sbuf_tensor(f"s{s}_r0f", [128, D], F32) as r0f,
        nc.sbuf_tensor(f"s{s}_dbf", [128, D], F32) as dbf,
        nc.sbuf_tensor(f"s{s}_f1f", [128, D], F32) as f1f,
        nc.sbuf_tensor(f"s{s}_xqt", [128, D], BF16) as xqt,
        nc.psum_tensor(f"s{s}_op0", [128, D], F32) as op0,
        nc.psum_tensor(f"s{s}_op1", [128, D], F32) as op1,
        nc.psum_tensor(f"s{s}_trp", [128, D], BF16) as trp,
        nc.psum_tensor(f"s{s}_ivt", [1, 128], F32) as ivt,
        nc.psum_tensor(f"s{s}_ivb", [128, 128], F32) as ivb,
    ):
        em.emit("gpsimd", lambda e: e.dma_start(
            rearr(wo, "p (c d) -> p c d", c=DC),
            rearr(dr["wo_t"], "(c p) d -> p c d", p=128)), inc=gw1, k=16,
            waits=[(ctx["sod"], ctx["attn_sod"]), (ctx["sqa"], None),
                   (ctx["ska"], None)])
        scr = {"i32": qi32[:], "r0": r0f[:], "d": dbf[:], "f1": f1f[:]}
        ops = (op0, op1)
        x_used = []
        evac_done = []
        spill_sp = []
        pend_spill = []
        for i in range(NT):
            xb = xint2[:, (i % 2) * D:(i % 2) * D + D]
            w = [(dv, x_used[i - 2])] if i >= 2 else []
            em.emit("sync", lambda e, i=i, xb=xb: e.dma_start(
                xb, dr["x"][s, i * 128:(i + 1) * 128, :]),
                inc=sx[i % 2], k=16, waits=w)
            ps = ops[i % 2]
            w = [(gw1, None), (ctx["sod"], ctx["attn_sod"])]
            if i >= 2:
                w.append((dv, evac_done[i - 2]))
            for c in range(DC):
                for nb in range(2):
                    em.emit("tensor", lambda e, ps=ps, i=i, c=c, nb=nb: e.matmul(
                        ps[:, nb * 512:(nb + 1) * 512],
                        attn[:, c * S + i * 128:c * S + i * 128 + 128],
                        wo[:, c * D + nb * 512:c * D + (nb + 1) * 512],
                        start=(c == 0), stop=(c == DC - 1)),
                        inc=pe if (c == DC - 1 and nb == 1) else None,
                        waits=w if (c == 0 and nb == 0) else [])
            em.emit("vector", lambda e, ps=ps: e.tensor_tensor(
                ao[:], ps[:], sb["bo"][:], ALU.add), waits=[(pe, None)])
            evac_done.append(dv.n)
            _ln_stats(ctx, ao[:])
            _ln_rsig(ctx)
            em.emit("vector", lambda e: e.tensor_scalar(
                yq[:], ao[:], sb["st2"][:, 0:1], sb["sdc"][:, 1:2],
                ALU.subtract, ALU.mult))
            x2b = x2t[:, (i % 2) * D:(i % 2) * D + D]
            xw = [(sx[i % 2], None)]
            if i >= 2:
                xw.append((ss[i % 2], spill_sp[i - 2]))
            em.emit("vector", lambda e, xb=xb, x2b=x2b: e.tensor_add(
                x2b, yq[:], xb), waits=xw)
            x_used.append(dv.n)
            pend_spill.append((i, x2b, dv.n))
            if len(pend_spill) > 1 or i == NT - 1:
                for (pi, px, pdv) in pend_spill[:None if i == NT - 1 else -1]:
                    em.emit("sync", lambda e, pi=pi, px=px: e.dma_start(
                        dr["x2scr"][s, pi * 128:(pi + 1) * 128, :], px),
                        inc=ss[pi % 2], k=16, waits=[(dv, pdv)])
                    spill_sp.append(ss[pi % 2].n)
                pend_spill[:] = ([] if i == NT - 1 else pend_spill[-1:])
            _ln_stats(ctx, x2b)
            _ln_rsig(ctx)
            em.emit("vector", lambda e, x2b=x2b: e.tensor_scalar(
                xn2[:], x2b, sb["st2"][:, 0:1], sb["sdc"][:, 1:2],
                ALU.subtract, ALU.mult))
            em.emit("vector", lambda e: e.reduce_max(
                sb["sdc"][:, 2:3], xn2[:], AX.X, apply_absolute_value=True))
            em.emit("vector", lambda e: e.tensor_scalar_max(
                sb["sdc"][:, 3:4], sb["sdc"][:, 2:3], 1e-5))
            em.emit("vector", lambda e: e.reciprocal(
                sb["sdc"][:, 4:5], sb["sdc"][:, 3:4]))
            em.emit("vector", lambda e: e.tensor_scalar_mul(
                sb["sdc"][:, 5:6], sb["sdc"][:, 4:5], 127.0))
            em.emit("vector", lambda e, i=i: e.tensor_scalar_mul(
                sb["invsx16"][:, i:i + 1], sb["sdc"][:, 3:4], 1.0 / 127.0))
            em.emit("vector", lambda e: e.tensor_scalar_mul(
                yq[:], xn2[:], sb["sdc"][:, 5:6]))
            _round_to_bf16(ctx, yq[:], xqt[:], scr, outw=[(pe, None)])
            for c in range(DC):
                em.emit("tensor", lambda e, c=c: e.transpose(
                    trp[:, c * 128:(c + 1) * 128], xqt[:, c * 128:(c + 1) * 128],
                    sb["eye16"][:]), inc=pe if c == DC - 1 else None,
                    waits=[(dv, None)] if c == 0 else [])
            em.emit("vector", lambda e, i=i: e.tensor_copy(
                rearr(xqfm, "p (c t) -> p c t", c=DC)[:, :, i * 128:(i + 1) * 128],
                rearr(trp, "p (c t) -> p c t", c=DC)), waits=[(pe, None)])
            em.emit("tensor", lambda e, i=i: e.transpose(
                ivt[0:1, :], sb["invsx16"][:, i:i + 1], sb["eye32"][:]),
                inc=pe, waits=[(dv, None)])
            em.emit("vector", lambda e: e.tensor_copy(sb["ivrow"][0:1, :],
                                                      ivt[0:1, :]),
                    waits=[(pe, None)])
            em.emit("tensor", lambda e: e.matmul(
                ivb[:], sb["ones32"][0:1, :], sb["ivrow"][0:1, :],
                start=True, stop=True), inc=pe, waits=[(dv, None)])
            em.emit("vector", lambda e, i=i: e.tensor_copy(
                ivfm[:, i * 128:(i + 1) * 128], ivb[:]), waits=[(pe, None)])
        em.play()


def _p45_wquant(ctx, s, w1q, w2q):
    nc, dr, sb, em = ctx["nc"], ctx["dr"], ctx["sb"], ctx["em"]
    dv, sws = ctx["dv"], ctx["sws"]
    sw = (ctx["sw0"], ctx["sw1"])
    if s == 1:
        em.emit("sync", lambda e: e.dma_start(w1q[:], dr["w1qscr"][:]),
                inc=sws, k=16,
                waits=[(sws, 32), (ctx["ss0"], None), (ctx["ss1"], None),
                       (ctx["sx0"], None), (ctx["sx1"], None)])
        em.emit("sync", lambda e: e.dma_start(w2q[:], dr["w2qscr"][:]),
                inc=sws, k=16)
        em.play()
        ctx["wq_rdy"] = sws.n
        return
    with (
        nc.sbuf_tensor(f"s{s}_wst", [128, 2 * 2048], F32) as wst,
        nc.sbuf_tensor(f"s{s}_c1b", [128, 2048], BF16) as c1b,
        nc.sbuf_tensor(f"s{s}_c2b", [128, 2048], BF16) as c2b,
    ):
        jobs = []
        for c in range(DC):
            for hcol in range(2):
                jobs.append(("w1_t", c * 128, hcol * 2048, 2048, 0,
                             w1q, c * DH + hcol * 2048))
        for c in range(HC):
            jobs.append(("w2_t", c * 128, 0, D, 1, w2q, c * D))
        qdone = []
        for j, (src, r0, c0, ncol, scol, dst, dcol) in enumerate(jobs):
            stg = wst[:, (j % 2) * 2048:(j % 2) * 2048 + ncol]
            w = [(dv, qdone[j - 2])] if j >= 2 else []
            if j == 0:
                w += [(ctx["ss0"], None), (ctx["ss1"], None),
                      (ctx["sx0"], None), (ctx["sx1"], None)]
            em.emit("sync", lambda e, stg=stg, src=src, r0=r0, c0=c0, ncol=ncol:
                    e.dma_start(stg, dr[src][r0:r0 + 128, c0:c0 + ncol]),
                    inc=sw[j % 2], k=16, waits=w)
            em.emit("vector", lambda e, stg=stg, ncol=ncol, scol=scol:
                    e.tensor_scalar(c1b[:, 0:ncol], stg,
                                    sb["swcol"][:, scol:scol + 1], 0.5,
                                    ALU.mult, ALU.is_ge),
                    waits=[(sw[j % 2], None)])
            em.emit("vector", lambda e, stg=stg, ncol=ncol, scol=scol:
                    e.tensor_scalar(c2b[:, 0:ncol], stg,
                                    sb["swcol"][:, scol:scol + 1], -0.5,
                                    ALU.mult, ALU.is_gt))
            em.emit("vector", lambda e, dst=dst, dcol=dcol, ncol=ncol:
                    e.scalar_tensor_tensor(dst[:, dcol:dcol + ncol],
                                           c1b[:, 0:ncol], -1.0,
                                           c2b[:, 0:ncol], ALU.add, ALU.add))
            qdone.append(dv.n)
        em.emit("sync", lambda e: e.dma_start(dr["w1qscr"][:], w1q[:]),
                inc=sws, k=16, waits=[(dv, None)])
        em.emit("sync", lambda e: e.dma_start(dr["w2qscr"][:], w2q[:]),
                inc=sws, k=16)
        em.play()
        ctx["wq_rdy"] = None


def _p5_ffn(ctx, s, xqfm, ivfm, w1q, w2q):
    nc, dr, sb, em = ctx["nc"], ctx["dr"], ctx["sb"], ctx["em"]
    pe, dv, ac = ctx["pe"], ctx["dv"], ctx["ac"]
    sf, sws = ctx["sf"], ctx["sws"]
    fspill = []
    with (
        nc.sbuf_tensor(f"s{s}_hgr", [128, 2 * 4 * 130], BF16) as hgr,
        nc.sbuf_tensor(f"s{s}_tmpg", [128, 512], BF16) as tmpg,
        nc.sbuf_tensor(f"s{s}_tcv", [128, HC * 128], BF16) as tcv,
        nc.sbuf_tensor(f"s{s}_gq", [128, 2 * 512], BF16) as gqb,
        nc.sbuf_tensor(f"s{s}_gmax", [128, 128], F32) as gmax,
        nc.sbuf_tensor(f"s{s}_sgb", [128, 128], F32) as sgb,
        nc.sbuf_tensor(f"s{s}_yb", [128, 2 * 512], F32) as yb,
        nc.sbuf_tensor(f"s{s}_qi2", [128, 512], I32) as qi2,
        nc.sbuf_tensor(f"s{s}_r02", [128, 512], F32) as r02,
        nc.sbuf_tensor(f"s{s}_db2", [128, 512], F32) as db2,
        nc.sbuf_tensor(f"s{s}_f12", [128, 512], F32) as f12,
        nc.sbuf_tensor(f"s{s}_ft", [128, D], F32) as ft,
        nc.psum_tensor(f"s{s}_hps0", [128, 130], F32) as hps0,
        nc.psum_tensor(f"s{s}_hps1", [128, 130], F32) as hps1,
        nc.psum_tensor(f"s{s}_fp0", [128, D], F32) as fp0,
        nc.psum_tensor(f"s{s}_fp1", [128, D], F32) as fp1,
        nc.psum_tensor(f"s{s}_tmp", [128, 128], F32) as tmps,
    ):
        hps = (hps0, hps1)
        fps = (fp0, fp1)
        amx_cm = nc.sbuf_tensor(f"s{s}_amx", [128, 1024], BF16)
        amx = amx_cm.__enter__()
        scr = {"i32": qi2[:], "r0": r02[:], "d": db2[:], "f1": f12[:]}
        fevac = []
        hevac = []
        convd = []
        w2done = {}
        for i in range(NT):
            j0 = 1 if i == 0 else 0
            j1 = 129 if i == NT - 1 else 130
            t0 = i * 128 - 1
            for cc in range(HC):
                idx = i * HC + cc
                g, c4 = cc // 4, cc % 4
                ps = hps[cc % 2]
                if idx >= 2:
                    w = [(dv, hevac[idx - 2])]
                else:
                    w = [(dv, None)]
                    if ctx.get("wq_rdy"):
                        w.append((sws, ctx["wq_rdy"]))
                for dc in range(DC):
                    em.emit("tensor", lambda e, ps=ps, cc=cc, dc=dc,
                            j0=j0, j1=j1, t0=t0: e.matmul(
                        ps[:, j0:j1],
                        w1q[:, dc * DH + cc * 128:dc * DH + cc * 128 + 128],
                        xqfm[:, dc * S + t0 + j0:dc * S + t0 + j1],
                        start=(dc == 0), stop=(dc == DC - 1)),
                        inc=pe if dc == DC - 1 else None,
                        waits=w if dc == 0 else [])
                gb = (g % 2) * 520
                hw = [(pe, None)]
                if len(convd) >= 2:
                    hw.append((dv, convd[-2]))
                em.emit("vector", lambda e, ps=ps, c4=c4, gb=gb, j0=j0, j1=j1,
                        t0=t0: e.scalar_tensor_tensor(
                    hgr[:, gb + c4 * 130 + j0:gb + c4 * 130 + j1],
                    ps[:, j0:j1], sb["iwcol"][:, 0:1],
                    ivfm[:, t0 + j0:t0 + j1], ALU.mult, ALU.mult),
                    waits=hw)
                if i == 0:
                    em.emit("vector", lambda e, c4=c4, gb=gb: e.memset(
                        hgr[:, gb + c4 * 130:gb + c4 * 130 + 1], 0.0))
                if i == NT - 1:
                    em.emit("vector", lambda e, c4=c4, gb=gb: e.memset(
                        hgr[:, gb + c4 * 130 + 129:gb + c4 * 130 + 130], 0.0))
                hevac.append(dv.n)
                if c4 == 3:
                    def hv(j, gb=gb):
                        return bass.AP(hgr, gb + j, [[1040, 128], [130, 4],
                                                     [1, 128]])
                    def wmap(j, g=g):
                        return bass.AP(sb["cw"], 3 * (4 * g) + j,
                                       [[3 * HC, 128], [3, 4], [0, 128]])
                    cbm = bass.AP(sb["cb"], 4 * g,
                                  [[HC, 128], [1, 4], [0, 128]])
                    out3 = bass.AP(tcv, g * 512, [[HC * 128, 128], [128, 4],
                                                  [1, 128]])
                    tm3 = bass.AP(tmpg, 0, [[512, 128], [128, 4], [1, 128]])
                    em.emit("vector", lambda e, hv=hv, wmap=wmap, out3=out3:
                            e.tensor_tensor(out3, hv(1), wmap(1), ALU.mult))
                    em.emit("vector", lambda e, cbm=cbm, out3=out3:
                            e.tensor_tensor(out3, out3, cbm, ALU.add))
                    em.emit("vector", lambda e, hv=hv, wmap=wmap, tm3=tm3:
                            e.tensor_tensor(tm3, hv(0), wmap(0), ALU.mult))
                    em.emit("vector", lambda e, out3=out3, tm3=tm3:
                            e.tensor_tensor(out3, out3, tm3, ALU.add))
                    em.emit("vector", lambda e, hv=hv, wmap=wmap, tm3=tm3:
                            e.tensor_tensor(tm3, hv(2), wmap(2), ALU.mult))
                    em.emit("vector", lambda e, out3=out3, tm3=tm3:
                            e.tensor_tensor(out3, out3, tm3, ALU.add))
                    convd.append(dv.n)
            em.emit("scalar", lambda e: e.activation(tcv[:], tcv[:], AF.Silu),
                    waits=[(dv, None)])
            for op, col in ((ALU.max, 0), (ALU.min, 1)):
                em.emit("vector", lambda e, op=op: e.tensor_tensor(
                    amx[:], tcv[:, 0:1024], tcv[:, 1024:2048], op),
                    waits=[(ac, None)])
                em.emit("vector", lambda e, op=op: e.scalar_tensor_tensor(
                    amx[:], tcv[:, 2048:3072], 1.0, amx[:], ALU.mult, op))
                em.emit("vector", lambda e, op=op: e.scalar_tensor_tensor(
                    amx[:], tcv[:, 3072:4096], 1.0, amx[:], ALU.mult, op))
                em.emit("vector", lambda e, op=op: e.tensor_tensor(
                    amx[:, 0:512], amx[:, 0:512], amx[:, 512:1024], op))
                em.emit("vector", lambda e, op=op: e.tensor_tensor(
                    amx[:, 0:256], amx[:, 0:256], amx[:, 256:512], op))
                em.emit("vector", lambda e, op=op, col=col: e.tensor_tensor(
                    gmax[:, 0:128] if col == 0 else f12[:, 0:128],
                    amx[:, 0:128], amx[:, 128:256], op))
            em.emit("vector", lambda e: e.scalar_tensor_tensor(
                gmax[:], f12[:, 0:128], -1.0, gmax[:], ALU.mult, ALU.max))
            em.emit("tensor", lambda e: e.transpose(
                tmps[:], gmax[:], sb["eye32"][:]), inc=pe,
                waits=[(dv, None), (dv, None)])
            em.emit("vector", lambda e: e.reduce_max(
                sb["sdc"][:, 2:3], tmps[:], AX.X), waits=[(pe, None)])
            em.emit("vector", lambda e: e.tensor_scalar_max(
                sb["sdc"][:, 3:4], sb["sdc"][:, 2:3], 1e-5))
            em.emit("vector", lambda e: e.reciprocal(
                sb["sdc"][:, 4:5], sb["sdc"][:, 3:4]))
            em.emit("vector", lambda e: e.tensor_scalar_mul(
                sb["sdc"][:, 5:6], sb["sdc"][:, 4:5], 127.0))
            em.emit("vector", lambda e: e.tensor_scalar_mul(
                sb["sdc"][:, 6:7], sb["sdc"][:, 3:4], 1.0 / 127.0))
            em.emit("tensor", lambda e: e.transpose(
                tmps[0:1, :], sb["sdc"][:, 5:6], sb["eye32"][:]), inc=pe,
                waits=[(dv, None)])
            em.emit("vector", lambda e: e.tensor_copy(
                sb["ivrow"][0:1, :], tmps[0:1, :]), waits=[(pe, None)])
            em.emit("tensor", lambda e: e.matmul(
                tmps[:], sb["ones32"][0:1, :], sb["ivrow"][0:1, :],
                start=True, stop=True), inc=pe, waits=[(dv, None)])
            em.emit("vector", lambda e: e.tensor_copy(sgb[:], tmps[:]),
                    waits=[(pe, None)])
            sgv = bass.AP(sgb, 0, [[128, 128], [0, 4], [1, 128]])
            silu_ac = ctx["ac"].n
            sgb_dv = dv.n
            del silu_ac
            silu_ac = ctx["ac"].n
            dve_rel = {}
            for g in range(8):
                gq4 = gqb[:, (g % 2) * 512:(g % 2) * 512 + 512]
                ybg = yb[:, (g % 2) * 512:(g % 2) * 512 + 512]
                ts = tcv[:, g * 512:(g + 1) * 512]
                yw = [(ac, silu_ac), (dv, dve_rel.get(g - 2, sgb_dv))]
                if i >= 1 and g == 0:
                    yw.append((sf, fspill[i - 1]))
                em.emit("gpsimd", lambda e, ts=ts, ybg=ybg: e.tensor_tensor(
                    rearr_ap3(ybg), rearr_ap3(ts), sgv, ALU.mult),
                    inc=ctx["gc"], waits=yw)
                yg = ctx["gc"].n
                em.emit("vector", lambda e, ybg=ybg: e.tensor_copy(qi2[:], ybg),
                        waits=[(ctx["gc"], yg)])
                em.emit("vector", lambda e: e.tensor_scalar(
                    r02[:], qi2[:], -1.0, None, ALU.add))
                em.emit("vector", lambda e, ybg=ybg: e.tensor_sub(
                    db2[:], ybg, r02[:]))
                dve_rel[g] = dv.n
                em.emit("vector", lambda e: e.scalar_tensor_tensor(
                    f12[:], db2[:], 1.5, r02[:], ALU.is_gt, ALU.add))
                ow = []
                if (i, g - 2) in w2done:
                    ow.append((pe, w2done[(i, g - 2)]))
                em.emit("vector", lambda e, gq4=gq4: e.scalar_tensor_tensor(
                    gq4, db2[:], 0.5, f12[:], ALU.is_ge, ALU.add), waits=ow)
                qd = dv.n
                ps = fps[i % 2]
                for cc4 in range(4):
                    cc = g * 4 + cc4
                    w2w = [(dv, qd)]
                    if i >= 2 and cc == 0:
                        w2w.append((dv, fevac[i - 2]))
                    for nb in range(2):
                        em.emit("tensor", lambda e, ps=ps, cc=cc, nb=nb,
                                gq4=gq4, cc4=cc4: e.matmul(
                            ps[:, nb * 512:(nb + 1) * 512],
                            gq4[:, cc4 * 128:(cc4 + 1) * 128],
                            w2q[:, cc * D + nb * 512:cc * D + (nb + 1) * 512],
                            start=(cc == 0), stop=(cc == HC - 1)),
                            inc=pe if nb == 1 else None,
                            waits=w2w if (nb == 0 and cc4 == 0) else [])
                w2done[(i, g)] = pe.n
            fw = [(pe, None)]
            if i >= 1:
                fw.append((sf, fspill[i - 1]))
            em.emit("vector", lambda e, i=i: e.tensor_scalar(
                ft[:], fps[i % 2][:], sb["sdc"][:, 6:7], sb["iwcol"][:, 1:2],
                ALU.mult, ALU.mult), waits=fw)
            fevac.append(dv.n)
            em.emit("gpsimd", lambda e, i=i: e.dma_start(
                dr["fscr"][i * 128:(i + 1) * 128, :], ft[:]),
                inc=sf, k=16, waits=[(dv, None)])
            fspill.append(sf.n)
        em.play()
        amx_cm.__exit__(None, None, None)

    with (
        nc.sbuf_tensor(f"s{s}_frl", [128, 2 * D], F32) as frl,
        nc.sbuf_tensor(f"s{s}_x2rl", [128, 2 * D], F32) as x2rl,
        nc.sbuf_tensor(f"s{s}_fn", [128, D], F32) as fn,
        nc.sbuf_tensor(f"s{s}_outt", [128, 2 * D], F32) as outt,
    ):
        srf = (ctx["srf0"], ctx["srf1"])
        srx = (ctx["srx0"], ctx["srx1"])
        so = (ctx["so0"], ctx["so1"])
        used = []
        outdma = []
        for i in range(NT):
            fb = frl[:, (i % 2) * D:(i % 2) * D + D]
            xb = x2rl[:, (i % 2) * D:(i % 2) * D + D]
            w = [(sf, fspill[NT - 1]),
                 (ctx["ss0"], None), (ctx["ss1"], None)]
            if i >= 2:
                w.append((dv, used[i - 2]))
            em.emit("sync", lambda e, i=i, fb=fb: e.dma_start(
                fb, dr["fscr"][i * 128:(i + 1) * 128, :]),
                inc=srf[i % 2], k=16, waits=w)
            em.emit("sync", lambda e, i=i, xb=xb: e.dma_start(
                xb, dr["x2scr"][s, i * 128:(i + 1) * 128, :]),
                inc=srx[i % 2], k=16)
            _ln_stats(ctx, fb, w=[(srf[i % 2], None)])
            _ln_rsig(ctx)
            em.emit("vector", lambda e, fb=fb: e.tensor_scalar(
                fn[:], fb, sb["st2"][:, 0:1], sb["sdc"][:, 1:2],
                ALU.subtract, ALU.mult))
            ow = [(srx[i % 2], None)]
            if i >= 2:
                ow.append((so[i % 2], outdma[i - 2]))
            em.emit("vector", lambda e, xb=xb, i=i: e.tensor_add(
                outt[:, (i % 2) * D:(i % 2) * D + D], fn[:], xb), waits=ow)
            used.append(dv.n)
            em.emit("gpsimd", lambda e, i=i: e.dma_start(
                dr["out"][s, i * 128:(i + 1) * 128, :],
                outt[:, (i % 2) * D:(i % 2) * D + D]),
                inc=so[i % 2], k=16, waits=[(dv, None)])
            outdma.append(so[i % 2].n)
        em.play()


def _host_inputs(inputs):
    f32 = np.float32
    wq = np.ascontiguousarray(
        np.asarray(inputs["L0_attn_wq"], f32).transpose(1, 0, 2).reshape(D, D))
    wk = np.ascontiguousarray(
        np.asarray(inputs["L0_attn_wk"], f32).transpose(1, 0, 2).reshape(D, D))
    wv = np.ascontiguousarray(
        np.asarray(inputs["L0_attn_wv"], f32).transpose(1, 0, 2).reshape(D, D))
    wo = np.ascontiguousarray(np.asarray(inputs["L0_attn_wo"], f32).T)
    w1 = np.ascontiguousarray(np.asarray(inputs["L0_ffn_w1"], f32).T)
    w2 = np.ascontiguousarray(np.asarray(inputs["L0_ffn_w2"], f32).T)
    bq = np.ascontiguousarray(
        np.asarray(inputs["L0_attn_bq"], f32).reshape(D).reshape(DC, 128).T)
    bk = np.ascontiguousarray(
        np.asarray(inputs["L0_attn_bk"], f32).reshape(D).reshape(DC, 128).T)
    bv = np.ascontiguousarray(np.broadcast_to(
        np.asarray(inputs["L0_attn_bv"], f32).reshape(1, D), (128, D)))
    bo = np.ascontiguousarray(np.broadcast_to(
        np.asarray(inputs["L0_attn_bo"], f32).reshape(1, D), (128, D)))
    cw = np.asarray(inputs["L0_ffn_conv_w"], f32).reshape(DH, 3)
    cw_l = np.ascontiguousarray(
        cw.reshape(HC, 128, 3).transpose(1, 0, 2).reshape(128, 3 * HC))
    cb_l = np.ascontiguousarray(
        np.asarray(inputs["L0_ffn_conv_b"], f32).reshape(HC, 128).T)
    eye16 = np.eye(128, dtype=ml_dtypes.bfloat16)
    eye32 = np.eye(128, dtype=f32)
    return {
        "wq_t": wq, "wk_t": wk, "wv_t": wv, "wo_t": wo, "w1_t": w1, "w2_t": w2,
        "bq_l": bq, "bk_l": bk, "bv_b": bv, "bo_b": bo,
        "cw_l": cw_l, "cb_l": cb_l, "eye16": eye16, "eye32": eye32,
    }


def kernel(**inputs):
    if "nc" not in _CACHE:
        _CACHE["nc"] = build_nc()
    nc = _CACHE["nc"]
    shared = _host_inputs(inputs)
    x = np.asarray(inputs["L0_attn_x"], np.float32)
    in_maps = []
    for c in range(NCORES):
        m = dict(shared)
        m["x"] = np.ascontiguousarray(x[c * SEQS:(c + 1) * SEQS])
        in_maps.append(m)
    res = run_bass_kernel_spmd(nc, in_maps, core_ids=list(range(NCORES)))
    return np.concatenate([r["out"] for r in res.results], axis=0)
